# version 30
# baseline (speedup 1.0000x reference)
import numpy as np
import ml_dtypes
import concourse.bass as bass
import concourse.mybir as mybir
import concourse.dve_ops as dve_ops
from concourse.bass_utils import run_bass_kernel_spmd
from contextlib import ExitStack

F32 = mybir.dt.float32
BF16 = mybir.dt.bfloat16
AF = mybir.ActivationFunctionType
ALU = mybir.AluOpType

S = 2048
D = 1024
NT = 8
STL = 16
CH = 4
DH = 64
NHP = 8
EPS_RMS = float(np.finfo(np.float32).eps)
EPS_LN = 1e-5
KB = 1024

ENGS = ("sync", "tensor", "scalar", "vector", "gpsimd")
DMA_SEMS = {"dc", "dt", "dx0", "dx1", "dx2", "dx3", "dxc", "dxc1", "dxc2", "dxc3", "dout", "do0", "do1", "do2",
            "dmag0", "dmag1", "df0", "df1", "dq0", "dq1", "dh", "dv", "dO", "dsw"}

_TAPS = False


def _consts():
    invf = (10000.0 ** (-(np.arange(0, DH, 2) / DH))).astype(np.float64)
    f = invf[np.arange(128) % 32]
    t = np.arange(S, dtype=np.float64)
    ang = f[:, None] * t[None, :]
    cost = np.cos(ang).astype(np.float32)
    sint = np.sin(ang).astype(np.float32)
    P = np.zeros((128, 128), np.float32)
    for i in range(64):
        P[2 * i, 2 * i + 1] = -1.0
        P[2 * i + 1, 2 * i] = 1.0
    pt = np.ascontiguousarray(P.T).astype(np.float32)
    tri1 = (np.arange(128)[None, :] >= np.arange(128)[:, None]).astype(np.float32)
    tri = np.concatenate([tri1, tri1], axis=1)
    emat = (np.arange(D)[None, :] // DH == np.arange(16)[:, None]).astype(np.float32)
    hpad = np.zeros((128, 31), np.float32)
    for p in range(128):
        hpad[p, 15 + (p >= 64)] = 1.0
    spad = np.zeros((16, 1152), np.float32)
    for h in range(16):
        base = (7 - h // 2) * 128 + 64 * (h % 2)
        spad[h, base:base + 64] = 1.0
    return dict(cost=cost, sint=sint, pt=pt, tri=tri, emat=emat, hpad=hpad, spad=spad)


class Builder:
    SERIAL = {"scalar": "s", "vector": "v", "gpsimd": "g"}

    def __init__(self, nc):
        self.nc = nc
        self.ops = {e: [] for e in ENGS}
        self.cnt = {}
        self.lastw = {e: {} for e in ENGS}
        self.sems = {}

    def sem(self, name, ctx):
        self.sems[name] = ctx.enter_context(self.nc.semaphore(name))
        self.cnt[name] = 0

    def op(self, eng, fn, waits=(), inc=None):
        own = self.SERIAL.get(eng)
        if own is not None and inc is None:
            inc = own
        if own is not None and self.cnt[own] > 0:
            waits = list(waits) + [(own, self.cnt[own])]
        wl = []
        for item in waits:
            if item is None:
                continue
            sname, v = item
            if v is None or v <= 0:
                continue
            if sname in DMA_SEMS:
                v = self.cnt[sname]
            if v > self.lastw[eng].get(sname, 0):
                self.lastw[eng][sname] = v
                wl.append((sname, v))
        amt = 0
        tot = None
        if inc is not None:
            amt = 16 if inc in DMA_SEMS else 1
            self.cnt[inc] += amt
            tot = self.cnt[inc]
        self.ops[eng].append((fn, tuple(wl), inc, amt))
        return tot

    def check(self):
        ptr = {e: 0 for e in ENGS}
        val = {s: 0 for s in self.cnt}
        total = sum(len(v) for v in self.ops.values())
        done = 0
        while done < total:
            prog = False
            for e in ENGS:
                while ptr[e] < len(self.ops[e]):
                    _, wl, inc, amt = self.ops[e][ptr[e]]
                    if all(val[s] >= v for s, v in wl):
                        if inc:
                            val[inc] += amt
                        ptr[e] += 1
                        done += 1
                        prog = True
                    else:
                        break
            if not prog:
                detail = {e: (ptr[e], len(self.ops[e]),
                              self.ops[e][ptr[e]][1] if ptr[e] < len(self.ops[e]) else None)
                          for e in ENGS}
                raise RuntimeError(f"deadlock: sems={val} at (ptr,len,waits)={detail}")

    def emit(self):
        sems = self.sems
        with self.nc.Block() as block:
            def runner(eng_ops):
                def body(e):
                    for fn, wl, inc, amt in eng_ops:
                        for sname, v in wl:
                            e.wait_ge(sems[sname], v)
                        r = fn(e)
                        if inc is not None:
                            r.then_inc(sems[inc], amt)
                return body
            block.sync(runner(self.ops["sync"]))
            block.tensor(runner(self.ops["tensor"]))
            block.scalar(runner(self.ops["scalar"]))
            block.vector(runner(self.ops["vector"]))
            block.gpsimd(runner(self.ops["gpsimd"]))


def build_nc():
    nc = bass.Bass("TRN2", target_bir_lowering=False)
    ctx = ExitStack()

    xt_h = nc.dram_tensor("xt", [D, S], F32, kind="ExternalInput").ap()
    w_h = {w: nc.dram_tensor(w, [D, D], F32, kind="ExternalInput").ap()
           for w in ("wqt", "wkt", "wvt", "wot", "wft")}
    b_h = {b: nc.dram_tensor(b, [D], F32, kind="ExternalInput").ap()
           for b in ("bq", "bk", "bv")}
    nw2_h = nc.dram_tensor("nw2", [128], F32, kind="ExternalInput").ap()
    nwp2_h = nc.dram_tensor("nwp2", [128], F32, kind="ExternalInput").ap()
    cost_h = nc.dram_tensor("cost", [128, S], F32, kind="ExternalInput").ap()
    sint_h = nc.dram_tensor("sint", [128, S], F32, kind="ExternalInput").ap()
    pt_h = nc.dram_tensor("pt", [128, 128], F32, kind="ExternalInput").ap()
    tri_h = nc.dram_tensor("tri", [128, 256], F32, kind="ExternalInput").ap()
    emat_h = nc.dram_tensor("emat", [16, D], F32, kind="ExternalInput").ap()
    hpad_h = nc.dram_tensor("hpad", [128, 31], F32, kind="ExternalInput").ap()
    spad_h = nc.dram_tensor("spad", [16, 1152], F32, kind="ExternalInput").ap()
    o_h = nc.dram_tensor("o", [CH * D, 512], F32, kind="ExternalOutput").ap()
    taps = {}
    if _TAPS:
        for nm, shp, dt in (("tap_q", [D, S], BF16), ("tap_k", [D, S], BF16),
                            ("tap_v", [S, D], BF16), ("tap_z", [D, S], BF16),
                            ("tap_u", [D, S], BF16), ("tap_rq", [16, S], BF16),
                            ("tap_rk", [16, S], BF16), ("tap_sum2x", [1, S], BF16)):
            taps[nm] = nc.dram_tensor(nm, shp, dt, kind="ExternalOutput").ap()

    SBUF_BASE = 16512
    def at(name, shape, dtype, off):
        end = off + SBUF_BASE + int(np.prod(shape[1:])) * (2 if dtype == BF16 else 4)
        assert end <= 229344, (name, end)
        return nc.alloc_sbuf_tensor_at(name, shape, dtype, offset=off + SBUF_BASE)

    XB = [at(f"xb{i}", [128, S], BF16, i * 4 * KB) for i in range(NT)]
    QT = [at(f"qt{i}", [128, S], BF16, 32 * KB + i * 4 * KB) for i in range(NT)]
    KT = [at(f"kt{i}", [128, S], BF16, 64 * KB + i * 4 * KB) for i in range(NT)]
    U = [at(f"u{i}", [128, 512], F32, 64 * KB + i * 2 * KB) for i in range(NT)]
    RO = 80 * KB
    MROW = at("mrow", [1, 512], F32, RO)
    MROW2 = at("mrow2", [1, 512], F32, RO + 2 * KB)
    MH = at("mh", [1, 512], BF16, RO + 4 * KB)
    MH2 = at("mh2", [1, 512], BF16, RO + 5 * KB)
    ML2 = at("ml2", [1, 512], BF16, RO + 6 * KB)
    ML = at("ml", [1, 512], BF16, RO + 7 * KB)
    VROW = at("vrow", [1, 512], F32, RO + 8 * KB)
    VROW2 = at("vrow2", [1, 512], F32, RO + 10 * KB)
    RVH = at("rvh", [1, 512], BF16, RO + 12 * KB)
    RVL = at("rvl", [1, 512], BF16, RO + 15 * KB)
    VT = [at(f"vt{i}", [128, D], BF16, 96 * KB + i * 2 * KB) for i in range(STL)]
    QRAW = at("qraw", [128, S], F32, 96 * KB)
    QRB = [at(f"qrb{i}", [128, 512], BF16, 104 * KB + i * KB) for i in range(4)]
    QSQ = [at(f"qsq{i}", [128, 512], BF16, 108 * KB + i * KB) for i in range(4)]
    T1 = [at(f"t1{i}", [128, 512], F32, 112 * KB + i * 2 * KB) for i in range(4)]
    T2 = [at(f"t2{i}", [128, 512], F32, 120 * KB + i * 2 * KB) for i in range(4)]
    SW = [at(f"sw{i}", [128, D], BF16, 96 * KB + i * 2 * KB) for i in range(NT)]
    WCT = [at(f"wct{i}", [128, 512], F32, 112 * KB + i * 2 * KB) for i in range(NT)]
    RQH = at("rqh", [16, S], BF16, 128 * KB)
    RQL = at("rql", [16, S], BF16, 132 * KB)
    RKH = at("rkh", [16, S], BF16, 136 * KB)
    RKL = at("rkl", [16, S], BF16, 140 * KB)
    SC1 = at("sc1", [16, 512], F32, 144 * KB)
    SC2 = at("sc2", [16, 512], F32, 146 * KB)
    PTC = at("ptc", [128, 128], BF16, 148 * KB)
    TRIC = at("tric", [128, 256], BF16, 148 * KB + 256)
    EMC = at("emc", [16, D], BF16, 148 * KB + 832)
    HPC = at("hpc", [128, 31], BF16, 148 * KB + 768)
    SPC = at("spc", [16, 1152], BF16, 150 * KB + 896)
    ONE1 = at("one1", [1, 128], BF16, 153 * KB + 256)
    ONEC = at("onec", [128, 1], BF16, 153 * KB + 512)
    TWO1 = at("two1", [128, 1], BF16, 153 * KB + 544)
    NEGM = at("negm", [1, 128], BF16, 153 * KB + 576)
    BQC = at("bqc", [128, NT], F32, 153 * KB + 832)
    BKC = at("bkc", [128, NT], F32, 153 * KB + 864)
    NWC = at("nwc", [128, 1], F32, 153 * KB + 896)
    NWPC = at("nwpc", [128, 1], F32, 153 * KB + 928)
    CSWOH = at("cswoh", [128, NT], BF16, 153 * KB + 960)
    CSWOL = at("cswol", [128, NT], BF16, 153 * KB + 992)
    CSWH = at("cswh", [128, NT], BF16, 154 * KB)
    CSWL = at("cswl", [128, NT], BF16, 154 * KB + 32)
    CSWO = at("cswo", [128, NT], F32, 154 * KB + 64)
    CSWOF = at("cswof", [128, NT], F32, 154 * KB + 96)
    EPSRC = at("epsrc", [16, 1], F32, 154 * KB + 128)
    SCL64 = at("scl64", [16, 1], F32, 154 * KB + 160)
    EPSK = at("epsk", [16, 1], F32, 154 * KB + 192)
    SCLK = at("sclk", [16, 1], F32, 154 * KB + 224)
    EPSLC = at("epslc", [1, 1], F32, 154 * KB + 256)
    SCL1K = at("scl1k", [1, 1], F32, 154 * KB + 288)
    NEG1C = at("neg1c", [16, 1], F32, 154 * KB + 320)
    NEGH = at("negh", [1, 1], F32, 154 * KB + 352)
    SUM2X = at("sum2x", [1, S], BF16, 156 * KB)
    XSTG = [at(f"xstg{i}", [128, S], F32, 160 * KB + i * 8 * KB) for i in range(2)]
    COST = at("costb", [128, S], F32, 160 * KB)
    SINT = at("sintb", [128, S], F32, 168 * KB)
    PATF = [at(f"patf{i}", [128, 1024], BF16, 160 * KB + i * 2 * KB) for i in range(6)]
    RSQB = [at(f"rsqb{i}", [128, 512], F32, 172 * KB + i * 2 * KB) for i in range(2)]
    XC = [at(f"xc{i}", [128, 512], F32, 160 * KB + i * 2 * KB) for i in range(NT)]
    WB = [at(f"wb{i}", [128, D], BF16, 176 * KB + i * 2 * KB) for i in range(NT)]
    WS = [at(f"ws{i}", [128, D], F32, 192 * KB + i * 4 * KB) for i in range(2)]
    WH = [at(f"wh{i}", [128, D], BF16, 192 * KB + i * 2 * KB) for i in range(7)]
    WS2 = at("ws2", [128, D], F32, 200 * KB)
    SGNT = at("sgnt", [128, D], BF16, 204 * KB)
    RVB2 = at("rvb2", [128, 512], F32, 192 * KB)
    SPST = at("spst", [16, 1152], F32, 200 * KB)
    BVB = at("bvb", [128, D], F32, 200 * KB)
    OUTS = [at(f"outs{i}", [128, 512], F32, 200 * KB + i * 2 * KB) for i in range(3)]
    VTMP = at("vtmp", [128, 512], F32, 204 * KB)
    WCSQ = [at(f"wcsq{i}", [128, 512], BF16, 144 * KB + i * KB) for i in range(2)]
    BVROW = at("bvrow", [1, D], F32, 96 * KB)
    BVH = at("bvh", [1, D], BF16, 100 * KB)
    BVL = at("bvl", [1, D], BF16, 102 * KB)
    BVHF = at("bvhf", [1, D], F32, 104 * KB)

    PAIR = [ctx.enter_context(nc.psum_tensor(f"pp{i}", [128, 1024], F32)) for i in range(3)]
    PZ = ctx.enter_context(nc.psum_tensor("pz", [128, 512], F32))
    PX = ctx.enter_context(nc.psum_tensor("px", [128, 512], F32))

    B = Builder(nc)
    for s in ("dc", "dt", "dx0", "dx1", "dx2", "dx3", "dxc", "dxc1", "dxc2", "dxc3", "dout",
              "do0", "do1", "do2", "dmag0", "dmag1", "df0", "df1", "dq0", "dq1",
              "dh", "dv", "dO", "dsw", "t", "s", "v", "g"):
        B.sem(s, ctx)
    op = B.op
    C = B.cnt

    op("sync", lambda e: e.dma_start(out=WS[0][:, 0:128], in_=pt_h), inc="dc")
    op("sync", lambda e: e.dma_start(out=WS[0][:, 128:384], in_=tri_h), inc="dc")
    op("sync", lambda e: e.dma_start(out=WS[0][:, 384:415], in_=hpad_h), inc="dc")
    op("sync", lambda e: e.dma_start(out=WS[1][0:16, :], in_=emat_h), inc="dc")
    op("sync", lambda e: e.dma_start(out=SPST[:], in_=spad_h), inc="dc")
    for src, dst in ((nw2_h, NWC), (nwp2_h, NWPC)):
        op("sync", lambda e, s=src, d=dst: e.dma_start(out=d[:], in_=s), inc="dc")
    for t in range(NT):
        op("sync", lambda e, t=t: e.dma_start(out=BQC[:, t:t + 1], in_=b_h["bq"][t * 128:(t + 1) * 128]), inc="dc")
        op("sync", lambda e, t=t: e.dma_start(out=BKC[:, t:t + 1], in_=b_h["bk"][t * 128:(t + 1) * 128]), inc="dc")
    consts_done = C["dc"]
    op("gpsimd", lambda e: e.tensor_copy(PTC[:], WS[0][:, 0:128]),
       waits=[("dc", consts_done)], inc="g")
    op("gpsimd", lambda e: e.tensor_copy(TRIC[:], WS[0][:, 128:384]), inc="g")
    op("gpsimd", lambda e: e.tensor_copy(HPC[:], WS[0][:, 384:415]), inc="g")
    op("gpsimd", lambda e: e.tensor_copy(EMC[:], WS[1][0:16, :]), inc="g")
    op("gpsimd", lambda e: e.tensor_copy(SPC[:], SPST[:]), inc="g")
    cconst_g = C["g"]
    op("gpsimd", lambda e: e.memset(ONE1[:], 1.0), inc="g")
    op("gpsimd", lambda e: e.memset(ONEC[:], 1.0), inc="g")
    op("gpsimd", lambda e: e.memset(TWO1[:], 2.0), inc="g")
    op("gpsimd", lambda e: e.memset(NEGM[:], -1.0 / 1024.0), inc="g")
    op("gpsimd", lambda e: e.memset(EPSRC[:], EPS_RMS), inc="g")
    op("gpsimd", lambda e: e.memset(SCL64[:], 1.0 / 64.0), inc="g")
    op("gpsimd", lambda e: e.memset(EPSK[:], 4096.0 * EPS_RMS), inc="g")
    op("gpsimd", lambda e: e.memset(SCLK[:], 64.0), inc="g")
    op("gpsimd", lambda e: e.memset(EPSLC[:], EPS_LN), inc="g")
    op("gpsimd", lambda e: e.memset(SCL1K[:], 1.0 / 1024.0), inc="g")
    op("gpsimd", lambda e: e.memset(NEG1C[:], -1.0), inc="g")
    op("gpsimd", lambda e: e.memset(NEGH[:], -0.5), inc="g")
    memset_done = C["g"]

    xcast0, xcast = [], []
    for i in range(NT):
        stg = XSTG[i % 2]
        dsem = "dx0" if i % 2 == 0 else "dx1"
        w = [("v", xcast0[i - 2])] if i >= 2 else []
        op("sync" if i % 2 == 0 else "gpsimd",
           lambda e, i=i, stg=stg: e.dma_start(
               out=stg[:, 0:512], in_=xt_h[i * 128:(i + 1) * 128, 0:512]),
           waits=w, inc=dsem)
        op("vector", lambda e, i=i, stg=stg: e.tensor_copy(
            XB[i][:, 0:512], stg[:, 0:512]), waits=[(dsem, 1)], inc="v")
        xcast0.append(C["v"])
    for i in range(NT):
        stg = XSTG[i % 2]
        dsem = "dx2" if i % 2 == 0 else "dx3"
        if i >= 2:
            w = [("v", xcast[i - 2])]
        else:
            w = [("v", xcast0[6 + i])]
        op("sync" if i % 2 == 0 else "gpsimd",
           lambda e, i=i, stg=stg: e.dma_start(
               out=stg[:, 512:S], in_=xt_h[i * 128:(i + 1) * 128, 512:S]),
           waits=w, inc=dsem)
        op("vector", lambda e, i=i, stg=stg: e.tensor_copy(
            XB[i][:, 512:S], stg[:, 512:S]), waits=[(dsem, 1)], inc="v")
        xcast.append(C["v"])
    xcast_done = C["v"]
    op("sync", lambda e: e.dma_start(out=COST[:], in_=cost_h), waits=[("v", xcast_done)], inc="dt")
    op("sync", lambda e: e.dma_start(out=SINT[:], in_=sint_h), inc="dt")
    tables_loaded = C["dt"]
    op("vector", lambda e: e.tensor_scalar_mul(COST[:], COST[:], NWC[:, 0:1]),
       waits=[("dt", tables_loaded), ("dc", consts_done)], inc="v")
    op("vector", lambda e: e.tensor_scalar_mul(SINT[:], SINT[:], NWPC[:, 0:1]), inc="v")
    tables_done = C["v"]

    def stat_ps(c):
        return PAIR[c // 2][0:1, (c % 2) * 512:(c % 2) * 512 + 512]
    for c in range(CH):
        for i in range(NT):
            xw = xcast0[i] if c == 0 else xcast[i]
            op("tensor", lambda e, c=c, i=i: e.matmul(
                stat_ps(c), TWO1[:, 0:1], XB[i][:, c * 512:(c + 1) * 512],
                start=(i == 0), stop=(i == NT - 1)),
               waits=[("v", xw), ("g", memset_done)],
               inc=("t" if i == NT - 1 else None))
        op("scalar", lambda e, c=c: e.copy(SUM2X[0:1, c * 512:(c + 1) * 512], stat_ps(c)),
           waits=[("t", C["t"])], inc="s")
    xstats_done_s = C["s"]

    def wdma(name, i, dst, dsem, waits=()):
        op("gpsimd", lambda e, name=name, i=i, dst=dst: e.dma_start(
            out=dst[:], in_=w_h[name][i * 128:(i + 1) * 128, :]),
           waits=waits, inc=dsem)

    for i in range(NT):
        wdma("wqt", i, WB[i], "dq0" if i < 4 else "dq1",
             waits=[("g", memset_done)] if i == 0 else ())

    def qk_phase(which, prev):
        dst = QT if which == 0 else KT
        bias = BQC if which == 0 else BKC
        rhi = RQH if which == 0 else RKH
        rlo = RQL if which == 0 else RKL
        epsc, sclc = (EPSRC, SCL64) if which == 0 else (EPSK, SCLK)
        if which == 1:
            wdma("wkt", 7, WB[7], "dq0", waits=[("t", prev["last_proj_t"])])
        NID = NT * CH
        mm_t, evac_s, sq_s, cast_v, tt1_v, tt2_v, tt3_g, rotmm, ssq_t = ({} for _ in range(9))
        sumsq_stop_t = [0] * CH
        epi_state = [prev["sc_free_v"]]

        def emit_epilogue(c):
            ps = PAIR[c // 2][0:16, (c % 2) * 512:(c % 2) * 512 + 512]
            op("scalar", lambda e, ps=ps: e.activation(
                SC1[:], ps, AF.Ln, bias=epsc[:, 0:1], scale=sclc[:, 0:1]),
               waits=[("t", sumsq_stop_t[c]), ("g", memset_done),
                      ("v", epi_state[0])], inc="s")
            op("scalar", lambda e: e.activation(SC2[:], SC1[:], AF.Exp, scale=NEG1C[:, 0:1]), inc="s")
            op("vector", lambda e, c=c: e.tensor_copy(
                rhi[:, c * 512:(c + 1) * 512], SC2[:]),
               waits=[("s", C["s"])], inc="v")
            op("vector", lambda e, c=c: e.tensor_copy(
                SC1[:], rhi[:, c * 512:(c + 1) * 512]), inc="v")
            op("vector", lambda e, c=c: e.tensor_sub(
                rlo[:, c * 512:(c + 1) * 512], SC2[:], SC1[:]), inc="v")
            epi_state[0] = C["v"]

        def emit_trailing(j):
            c, ot = divmod(j, NT)
            pr = (PZ[:, 0:512], PX[:, 0:512])[j % 2]
            op("tensor", lambda e, j=j, pr=pr: e.matmul(
                pr, PTC[:], QRB[j % 4][:], start=True, stop=True),
               waits=[("s", cast_v[j]), ("g", cconst_g),
                      ("v", tt2_v.get(j - 2, prev["v"] if j < 2 else 0))], inc="t")
            rotmm[j] = C["t"]
            ssw = [("s", sq_s[j])]
            if ot == 0:
                ssw.append(("s", prev["pair_free_s"]))
                ssw.append(("g", cconst_g))
            op("tensor", lambda e, j=j, ot=ot, c=c: e.matmul(
                PAIR[c // 2][0:16, (c % 2) * 512:(c % 2) * 512 + 512],
                HPC[:, 15 - 2 * ot:31 - 2 * ot], QSQ[j % 4][:],
                start=(ot == 0), stop=(ot == NT - 1)),
               waits=ssw, inc="t")
            ssq_t[j] = C["t"]
            if ot == NT - 1:
                sumsq_stop_t[c] = C["t"]
            op("vector", lambda e, j=j, c=c, pr=pr: e.tensor_mul(
                T2[j % 4][:], pr, SINT[:, c * 512:(c + 1) * 512]),
               waits=[("t", rotmm[j]),
                      ("g", tt3_g.get(j - 4, prev["g"] if j < 4 else 0))], inc="v")
            tt2_v[j] = C["v"]
            op("gpsimd", lambda e, j=j, ot=ot, c=c, dst=dst: e.tensor_add(
                dst[ot][:, c * 512:(c + 1) * 512], T1[j % 4][:], T2[j % 4][:]),
               waits=[("v", tt2_v[j]), ("v", tt1_v[j])], inc="g")
            tt3_g[j] = C["g"]

        for idx in range(NID):
            c, ot = divmod(idx, NT)
            pp = PAIR[2][:, (idx % 2) * 512:(idx % 2) * 512 + 512]
            w0 = [("s", evac_s.get(idx - 2, prev["pp_free_s"][idx % 2]))]
            for i in range(NT):
                wt = WB[i] if (which == 0 or i == 7) else WH[i]
                if which == 0:
                    dws = (("dq0" if i < 4 else "dq1"), 1)
                else:
                    dws = (("dq0" if i == 7 else "dh"), 1)
                xw = ("v", xcast0[i] if c == 0 else xcast[i])
                op("tensor", lambda e, i=i, ot=ot, c=c, pp=pp, wt=wt: e.matmul(
                    pp, wt[:, ot * 128:(ot + 1) * 128],
                    XB[i][:, c * 512:(c + 1) * 512],
                    start=(i == 0), stop=(i == NT - 1)),
                   waits=([*w0, dws, xw] if i == 0 else [dws, xw]),
                   inc=("t" if i == NT - 1 else None))
            mm_t[idx] = C["t"]
            sl = (idx % 4) * 512
            op("scalar", lambda e, sl=sl, ot=ot, pp=pp, bias=bias: e.activation(
                QRAW[:, sl:sl + 512], pp, AF.Identity,
                bias=bias[:, ot:ot + 1], scale=1.0),
               waits=[("t", mm_t[idx]), ("dc", consts_done),
                      ("v", tt1_v.get(idx - 4, prev["v"] if idx < 4 else 0))], inc="s")
            evac_s[idx] = C["s"]
            op("scalar", lambda e, idx=idx, sl=sl: e.square(
                QSQ[idx % 4][:], QRAW[:, sl:sl + 512]),
               waits=[("t", ssq_t.get(idx - 4, prev["t"] if idx < 4 else 0))], inc="s")
            sq_s[idx] = C["s"]
            op("scalar", lambda e, idx=idx, sl=sl: e.copy(
                QRB[idx % 4][:], QRAW[:, sl:sl + 512]),
               waits=[("t", rotmm.get(idx - 4, prev["t"] if idx < 4 else 0))], inc="s")
            cast_v[idx] = C["s"]
            op("vector", lambda e, idx=idx, sl=sl, c=c: e.tensor_mul(
                T1[idx % 4][:], QRAW[:, sl:sl + 512],
                COST[:, c * 512:(c + 1) * 512]),
               waits=[("v", tables_done), ("s", evac_s[idx]),
                      ("g", tt3_g.get(idx - 4, prev["g"] if idx < 4 else 0))], inc="v")
            tt1_v[idx] = C["v"]
            if idx >= 2:
                emit_trailing(idx - 2)
                if (idx - 2) % NT == NT - 1 and (idx - 2) // NT < CH - 1:
                    emit_epilogue((idx - 2) // NT)
            if idx == 2:
                if which == 0:
                    for i in range(7):
                        wdma("wkt", i, WH[i], "dh")
                else:
                    for i in range(7):
                        wdma("wvt", i, WB[i], "dv",
                             waits=[("t", prev["last_proj_t"])] if i == 0 else ())
        emit_trailing(NID - 2)
        emit_trailing(NID - 1)
        emit_epilogue(CH - 1)
        return dict(v=C["v"], t=C["t"], s=C["s"], g=C["g"],
                    pp_free_s=(evac_s[NID - 2], evac_s[NID - 1]),
                    pair_free_s=C["s"], sc_free_v=epi_state[0],
                    last_proj_t=mm_t[NID - 1],
                    tile_g={tt: tt3_g[(CH - 1) * NT + tt] for tt in range(NT)})

    prev0 = dict(t=0, v=0, s=0, g=memset_done, pp_free_s=(0, 0),
                 pair_free_s=xstats_done_s, sc_free_v=0)
    qres = qk_phase(0, prev0)
    kres = qk_phase(1, qres)

    wdma("wvt", 7, WB[7], "dq1", waits=[("t", kres["last_proj_t"])])
    op("sync", lambda e: e.dma_start(out=BVROW[:], in_=b_h["bv"][:]),
       waits=[("v", kres["v"])], inc="dt")
    op("vector", lambda e: e.tensor_copy(BVH[:], BVROW[:]), waits=[("dt", 1)], inc="v")
    op("vector", lambda e: e.tensor_sub(BVL[:], BVROW[:], BVH[:]), inc="v")
    bv_v = C["v"]
    vmm_t, rmm_t, vtta_v, vttb_v = {}, {}, {}, {}
    vpool = [PAIR[2][:, 0:512], PAIR[2][:, 512:1024], PZ[:, 0:512], PX[:, 0:512]]

    def emit_v_trailing(st):
        for half in range(2):
            pr = PAIR[half][:, 0:512]
            w = [("v", kres["v"]), ("g", cconst_g),
                 ("v", vttb_v.get((st - 1, half), 0))]
            if st == 0:
                w.append(("s", bvb_s))
            op("tensor", lambda e, st=st, half=half, pr=pr: e.matmul(
                pr, RKH[:, st * 128:(st + 1) * 128],
                EMC[:, half * 512:half * 512 + 512], start=True, stop=False),
               waits=w, inc=None)
            op("tensor", lambda e, st=st, half=half, pr=pr: e.matmul(
                pr, RKL[:, st * 128:(st + 1) * 128],
                EMC[:, half * 512:half * 512 + 512], start=False, stop=True), inc="t")
            rmm_t[(st, half)] = C["t"]
            pp = vpool[(2 * st + half) % 4]
            op("vector", lambda e, st=st, half=half, pp=pp: e.tensor_add(
                VTMP[:], pp, BVB[:, half * 512:half * 512 + 512]),
               waits=[("t", vmm_t[(st, half)]), ("s", bvb_s)], inc="v")
            vtta_v[(st, half)] = C["v"]
            op("vector", lambda e, st=st, half=half, pr=pr: e.tensor_mul(
                VT[st][:, half * 512:half * 512 + 512], VTMP[:], pr),
               waits=[("t", rmm_t[(st, half)])], inc="v")
            vttb_v[(st, half)] = C["v"]

    bvb_emitted = [False]

    def emit_bvb():
        for half in range(2):
            op("tensor", lambda e, half=half: e.matmul(
                PAIR[half][:, 512:1024], ONE1[:],
                BVH[0:1, half * 512:half * 512 + 512], start=True, stop=False),
               waits=[("v", bv_v), ("s", kres["s"])], inc=None)
            op("tensor", lambda e, half=half: e.matmul(
                PAIR[half][:, 512:1024], ONE1[:],
                BVL[0:1, half * 512:half * 512 + 512], start=False, stop=True), inc="t")
        bvmm = C["t"]
        op("scalar", lambda e: e.copy(BVB[:, 0:512], PAIR[0][:, 512:1024]),
           waits=[("t", bvmm)], inc="s")
        op("scalar", lambda e: e.copy(BVB[:, 512:1024], PAIR[1][:, 512:1024]), inc="s")
        return C["s"]

    for st in range(STL):
        if st >= 2:
            emit_v_trailing(st - 2)
        for half in range(2):
            pp = vpool[(2 * st + half) % 4]
            if st < 2:
                if (2 * st + half) % 4 < 2:
                    w2 = [("s", kres["pp_free_s"][(2 * st + half) % 4])]
                else:
                    w2 = [("v", kres["v"])]
            else:
                w2 = [("v", vtta_v[(st - 2, half)])]
            for i in range(NT):
                op("tensor", lambda e, st=st, half=half, i=i, pp=pp: e.matmul(
                    pp, XB[i][:, st * 128:(st + 1) * 128],
                    WB[i][:, half * 512:half * 512 + 512],
                    start=(i == 0), stop=(i == NT - 1)),
                   waits=([*w2, ("dq1" if i == 7 else "dv", 1)] if i == 0
                          else [("dq1" if i == 7 else "dv", 1)]),
                   inc=("t" if i == NT - 1 else None))
            vmm_t[(st, half)] = C["t"]
        if st == 0:
            bvb_s = emit_bvb()
    emit_v_trailing(STL - 2)
    emit_v_trailing(STL - 1)
    last_vproj_t = vmm_t[(STL - 1, 1)]
    v_done_v = C["v"]
    v_done_t = C["t"]

    if _TAPS:
        op("sync", lambda e: e.dma_start(out=taps["tap_rq"], in_=RQH[:]),
           waits=[("v", v_done_v)], inc="dout")
        op("sync", lambda e: e.dma_start(out=taps["tap_rk"], in_=RKH[:]), inc="dout")
        op("sync", lambda e: e.dma_start(out=taps["tap_sum2x"], in_=SUM2X[:]), inc="dout")
        for i in range(NT):
            op("sync", lambda e, i=i: e.dma_start(out=taps["tap_q"][i * 128:(i + 1) * 128, :], in_=QT[i][:]),
               waits=[("v", v_done_v), ("g", kres["tile_g"][i])], inc="dout")
            op("sync", lambda e, i=i: e.dma_start(out=taps["tap_k"][i * 128:(i + 1) * 128, :], in_=KT[i][:]),
               inc="dout")
        for i in range(STL):
            op("sync", lambda e, i=i: e.dma_start(out=taps["tap_v"][i * 128:(i + 1) * 128, :], in_=VT[i][:]),
               inc="dout")
    taps_qkv_dma = C["dout"]

    for i in range(NT):
        wdma("wot", i, WB[i], "dO",
             waits=[("t", last_vproj_t)] if i == 0 else ())
    cswo_v = None

    pat_ready, pat_map, pat_slot, pat_evac = {}, {}, {}, {}
    patf_user = [("g", 0)] * 6
    pair_user = [("v", v_done_v)] * 3
    jj = [0]
    zevac_v = {}
    rsqb_s = [0, 0]
    rsqb_reader = [0, 0]
    items = [(Q, t, j) for Q in range(CH) for t in range(NHP) for j in range(4 * (Q + 1))]

    def emit_scores(it):
        Q, t, j = it
        r = j * 128 - Q * 512
        off = max(0, r)
        pair = PAIR[jj[0] % 3]
        w = [("g", kres["tile_g"][t])]
        prev3 = items[jj[0] - 3] if jj[0] >= 3 else None
        if prev3 is not None:
            w.append(pat_evac[prev3])
        else:
            w.append(pair_user[jj[0] % 3])
        q0, q1 = Q * 512 + off, (Q + 1) * 512
        k0, k1 = j * 128, (j + 1) * 128
        op("tensor", lambda e, t=t, off=off, pair=pair, q0=q0, q1=q1, k0=k0, k1=k1: e.matmul(
            pair[:, off:512], KT[t][0:64, k0:k1], QT[t][0:64, q0:q1],
            start=True, stop=True, tile_position=(0, 0)), waits=w, inc=None)
        op("tensor", lambda e, t=t, off=off, pair=pair, q0=q0, q1=q1, k0=k0, k1=k1: e.matmul(
            pair[:, 512 + off:1024], KT[t][64:128, k0:k1], QT[t][64:128, q0:q1],
            start=True, stop=True, tile_position=(64, 0)), inc="t")
        sc_t = C["t"]
        pj = jj[0] % 6
        buf = PATF[pj]
        pat_map[it] = buf
        pat_slot[it] = pj
        bw = [patf_user[pj], ("v", kres["v"])]
        if jj[0] % 8 in (1, 2, 4, 6, 7):
            if off == 0:
                op("scalar", lambda e, buf=buf, pair=pair: e.activation(
                    buf[:], pair[:], AF.Square), waits=[("t", sc_t), *bw], inc="s")
            else:
                op("scalar", lambda e, buf=buf, pair=pair, off=off: e.activation(
                    buf[:, off:512], pair[:, off:512], AF.Square),
                   waits=[("t", sc_t), *bw], inc=None)
                op("scalar", lambda e, buf=buf, pair=pair, off=off: e.activation(
                    buf[:, 512 + off:1024], pair[:, 512 + off:1024], AF.Square), inc="s")
            ready = ("s", C["s"])
        else:
            if off == 0:
                op("vector", lambda e, buf=buf, pair=pair: e.tensor_copy(buf[:], pair[:]),
                   waits=[("t", sc_t), *bw], inc=None)
                op("vector", lambda e, buf=buf: e.tensor_mul(buf[:], buf[:], buf[:]), inc="v")
            else:
                op("vector", lambda e, buf=buf, pair=pair, off=off: e.tensor_copy(
                    buf[:, off:512], pair[:, off:512]), waits=[("t", sc_t), *bw], inc=None)
                op("vector", lambda e, buf=buf, pair=pair, off=off: e.tensor_copy(
                    buf[:, 512 + off:1024], pair[:, 512 + off:1024]), inc=None)
                op("vector", lambda e, buf=buf, off=off: e.tensor_mul(
                    buf[:, off:512], buf[:, off:512], buf[:, off:512]), inc=None)
                op("vector", lambda e, buf=buf, off=off: e.tensor_mul(
                    buf[:, 512 + off:1024], buf[:, 512 + off:1024],
                    buf[:, 512 + off:1024]), inc="v")
            ready = ("v", C["v"])
        pair_user[jj[0] % 3] = ready
        pat_evac[it] = ready
        if r >= 0:
            op("gpsimd", lambda e, buf=buf, r=r: e.tensor_mul(
                buf[:, r:r + 128], buf[:, r:r + 128], TRIC[:, 0:128]),
               waits=[ready, ("g", cconst_g)], inc=None)
            op("gpsimd", lambda e, buf=buf, r=r: e.tensor_mul(
                buf[:, 512 + r:512 + r + 128], buf[:, 512 + r:512 + r + 128],
                TRIC[:, 0:128]), inc="g")
            ready = ("g", C["g"])
        pat_ready[it] = ready
        jj[0] += 1

    bank_free = {}
    pending = []

    def flush_bcast():
        Q, t, ck, zp, copy_v = pending.pop()
        bcw = [("v", qres["v"]), ("s", rsqb_s[(t + 1) % 2]), ("g", cconst_g),
               copy_v]
        op("tensor", lambda e, t=t, Q=Q, zp=zp: e.matmul(
            zp[:], SPC[:, (7 - t) * 128:(8 - t) * 128], RQH[:, Q * 512:(Q + 1) * 512],
            start=True, stop=False), waits=bcw, inc=None)
        op("tensor", lambda e, t=t, Q=Q, zp=zp: e.matmul(
            zp[:], SPC[:, (7 - t) * 128:(8 - t) * 128], RQL[:, Q * 512:(Q + 1) * 512],
            start=False, stop=True), inc="t")
        bc_t = C["t"]
        op("scalar", lambda e, t=t, zp=zp: e.copy(RSQB[t % 2][:], zp[:]),
           waits=[("t", bc_t), ("v", rsqb_reader[t % 2]), ("v", kres["v"])], inc="s")
        rsqb_s[t % 2] = C["s"]
        bank_free[ck] = ("s", C["s"])
        op("vector", lambda e, t=t, Q=Q: e.tensor_mul(
            QT[t][:, Q * 512:(Q + 1) * 512], QT[t][:, Q * 512:(Q + 1) * 512],
            RSQB[t % 2][:]),
           waits=[("s", rsqb_s[t % 2])], inc="v")
        rsqb_reader[t % 2] = C["v"]

    def emit_z(it):
        Q, t, j = it
        ck = Q * NHP + t
        zp = PZ if ck % 2 == 0 else PX
        nk = 4 * (Q + 1)
        r = j * 128 - Q * 512
        off = max(0, r)
        buf = pat_map[it]
        ready = pat_ready[it]
        zw = [ready, ("v", vttb_v[(j, 0)]), ("v", vttb_v[(j, 1)])]
        if j == 0:
            zw.append(bank_free.get(ck - 2))
            zw.append(zevac_v.get(ck - 2))
            if Q == 0 and t == 0:
                zw.append(("v", v_done_v))
        if j == 3 and pending:
            flush_bcast()
        op("tensor", lambda e, t=t, j=j, off=off, buf=buf, nk=nk, zp=zp: e.matmul(
            zp[0:64, off:512], VT[j][:, t * 128:t * 128 + 64], buf[:, off:512],
            start=(j == 0), stop=(j == nk - 1), tile_position=(0, 0)),
           waits=zw, inc=None)
        op("tensor", lambda e, t=t, j=j, off=off, buf=buf, nk=nk, zp=zp: e.matmul(
            zp[64:128, off:512], VT[j][:, t * 128 + 64:(t + 1) * 128],
            buf[:, 512 + off:1024],
            start=(j == 0), stop=(j == nk - 1), tile_position=(0, 64),
            skip_group_check=True), inc="t")
        patf_user[pat_slot[it]] = ("t", C["t"])
        if j == nk - 1:
            op("scalar", lambda e, t=t, Q=Q, zp=zp: e.copy(
                QT[t][:, Q * 512:(Q + 1) * 512], zp[:]),
               waits=[("t", C["t"])], inc="s")
            zevac_v[ck] = ("s", C["s"])
            pending.append((Q, t, ck, zp, ("s", C["s"])))

    tmps_s = {}
    csw_v_holder = [None]
    for i, it in enumerate(items):
        if 20 <= i < 28:
            op("vector", lambda e, i=i: e.reduce_sum(
                CSWO[:, i - 20:i - 19], WB[i - 20][:], axis=mybir.AxisListType.X),
               waits=[("dO", 1)], inc="v")
        if i == 28:
            op("vector", lambda e: e.tensor_copy(CSWOH[:], CSWO[:]), inc="v")
            op("vector", lambda e: e.tensor_sub(CSWOL[:], CSWO[:], CSWOH[:]), inc="v")
            cswo_v = C["v"]
        if i in (32, 36):
            fi = (i - 32) // 4
            dsem = "df0" if fi % 2 == 0 else "df1"
            wfw = [("t", last_vproj_t), ("v", v_done_v)] if fi == 0 else []
            op("gpsimd" if fi % 2 == 0 else "sync",
               lambda e, fi=fi: e.dma_start(
                   out=WS[fi % 2][:], in_=w_h["wft"][fi * 128:(fi + 1) * 128, :]),
               waits=wfw, inc=dsem)
        if 48 <= i < 80 and i % 4 == 0:
            fi = (i - 48) // 4
            dsem = "df0" if fi % 2 == 0 else "df1"
            op("scalar", lambda e, fi=fi: e.sign(SGNT[:], WS[fi % 2][:]),
               waits=[(dsem, 1), ("v", C["v"])], inc="s")
            tmps_s[fi] = C["s"]
            op("vector", lambda e, fi=fi: e.reduce_sum(
                CSWO[:, fi:fi + 1], SGNT[:], axis=mybir.AxisListType.X),
               waits=[("s", tmps_s[fi]), ("v", cswo_v)], inc="v")
            if fi + 2 < NT:
                fj = fi + 2
                dsem2 = "df0" if fj % 2 == 0 else "df1"
                op("gpsimd" if fj % 2 == 0 else "sync",
                   lambda e, fj=fj: e.dma_start(
                       out=WS[fj % 2][:], in_=w_h["wft"][fj * 128:(fj + 1) * 128, :]),
                   waits=[("s", tmps_s[fi])], inc=dsem2)
        if i == 80:
            op("vector", lambda e: e.tensor_copy(CSWH[:], CSWO[:]), inc="v")
            op("vector", lambda e: e.tensor_sub(CSWL[:], CSWO[:], CSWH[:]), inc="v")
            csw_v_holder[0] = C["v"]
        emit_scores(it)
        if i >= 3:
            emit_z(items[i - 3])
    for it in items[-3:]:
        emit_z(it)
    flush_bcast()
    for fi in range(NT):
        op("gpsimd", lambda e, fi=fi: e.dma_start(
            out=SW[fi][:], in_=w_h["wft"][fi * 128:(fi + 1) * 128, :]),
           waits=[("t", C["t"])] if fi == 0 else (), inc="dsw")
    attn_done_v = C["v"]
    attn_done_t = C["t"]

    if _TAPS:
        for i in range(NT):
            op("sync", lambda e, i=i: e.dma_start(out=taps["tap_z"][i * 128:(i + 1) * 128, :], in_=QT[i][:]),
               waits=[("v", attn_done_v)], inc="dout")

    pw_pool = [PAIR[i][:, h * 512:h * 512 + 512] for i in range(3) for h in range(2)]
    pw_free = {k: pair_user[k // 2] for k in range(6)}
    st34 = dict(ffn_casts=None, csw_v=None, pzrow_free_s=0, mh_guard_t=0,
                m2_guard_t=0, m2chain_v=0, rvchain_v=0, rvb_reader_v=0,
                rvb_reader_g=0, rvb_reader_vm=0, var_free_s=0, rvb_g={})
    u_v, ubf_v, wc_v, m2_v_q, var_t_q = {}, {}, {}, {}, {}
    wcsq_reader_t = [0, 0]
    pwi = [0]

    def next_pw():
        k = pwi[0] % 6
        pwi[0] += 1
        return k

    def stageA(Q, cprev=None):
        xc_sem = []
        for i in range(NT):
            if Q == 0:
                w = [("t", attn_done_t), ("v", attn_done_v)]
            else:
                w = [("v", u_v[(Q - 1, i)])]
            ds = ("dxc", "dxc1", "dxc2", "dxc3")[i % 4]
            op("sync" if i % 2 == 0 else "gpsimd",
               lambda e, i=i, Q=Q: e.dma_start(
                   out=XC[i][:], in_=xt_h[i * 128:(i + 1) * 128, Q * 512:(Q + 1) * 512]),
               waits=w, inc=ds)
            xc_sem.append((ds, 1))
        for i in range(NT):
            w = [("v", attn_done_v), ("v", cswo_v), ("s", st34["pzrow_free_s"])]
            if Q == 0 and i == 0:
                w.append(("s", rsqb_s[0]))
                w.append(("s", rsqb_s[1]))
            op("tensor", lambda e, i=i, Q=Q: e.matmul(
                PX[0:1, 0:512], CSWOH[:, i:i + 1], QT[i][:, Q * 512:(Q + 1) * 512],
                start=(i == 0), stop=False), waits=(w if i == 0 else []), inc=None)
            op("tensor", lambda e, i=i, Q=Q: e.matmul(
                PX[0:1, 0:512], CSWOL[:, i:i + 1], QT[i][:, Q * 512:(Q + 1) * 512],
                start=False, stop=False), inc=None)
        op("tensor", lambda e, Q=Q: e.matmul(
            PX[0:1, 0:512], ONE1[0:1, 0:1], SUM2X[0:1, Q * 512:(Q + 1) * 512],
            start=False, stop=True), inc="t")
        stat_t = C["t"]
        op("scalar", lambda e: e.copy(MROW2[:], PX[0:1, 0:512]),
           waits=[("t", stat_t), ("v", st34["m2chain_v"])], inc="s")
        st34["pzrow_free_s"] = C["s"]
        op("vector", lambda e: e.tensor_copy(MH[:], MROW2[:]),
           waits=[("s", C["s"]), ("t", st34["mh_guard_t"])], inc="v")
        mh_v = C["v"]
        op("vector", lambda e: e.tensor_sub(ML[:], MROW2[:], MH[:]), inc="v")
        m_v = C["v"]
        for ot in range(NT):
            pk = next_pw()
            pw = pw_pool[pk]
            w = [pw_free[pk], ("v", attn_done_v)]
            for i in range(NT):
                op("tensor", lambda e, i=i, ot=ot, Q=Q, pw=pw: e.matmul(
                    pw, WB[i][:, ot * 128:(ot + 1) * 128], QT[i][:, Q * 512:(Q + 1) * 512],
                    start=(i == 0), stop=False),
                   waits=([*w, ("dO", 1)] if i == 0 else [("dO", 1)]),
                   inc=None)
            op("tensor", lambda e, pw=pw: e.matmul(
                pw, NEGM[:], MH[:], start=False, stop=False),
               waits=[("v", mh_v)], inc=None)
            op("tensor", lambda e, pw=pw: e.matmul(
                pw, NEGM[:], ML[:], start=False, stop=True),
               waits=[("v", m_v)], inc="t")
            wo_t = C["t"]
            st34["mh_guard_t"] = wo_t
            op("vector", lambda e, ot=ot, pw=pw: e.scalar_tensor_tensor(
                U[ot][:], XC[ot][:], 2.0, pw, op0=ALU.mult, op1=ALU.add),
               waits=[("t", wo_t), xc_sem[ot]], inc="v")
            u_v[(Q, ot)] = C["v"]
            op("vector", lambda e, ot=ot, Q=Q: e.tensor_copy(
                XB[ot][:, Q * 512:(Q + 1) * 512], U[ot][:]), inc="v")
            ubf_v[(Q, ot)] = C["v"]
            pw_free[pk] = ("v", u_v[(Q, ot)])
            if cprev is not None and ot == 1:
                stageC(cprev)
            if Q >= 1 and ot >= 3:
                emit_b_stat(Q, ot - 3)
        if Q >= 1:
            for i in range(5, NT):
                emit_b_stat(Q, i)
        if True:
            pass
        if _TAPS:
            for i in range(NT):
                op("sync", lambda e, i=i, Q=Q: e.dma_start(
                    out=taps["tap_u"][i * 128:(i + 1) * 128, Q * 512:(Q + 1) * 512],
                    in_=XB[i][:, Q * 512:(Q + 1) * 512]),
                   waits=[("v", ubf_v[(Q, i)])], inc="dout")
        if st34["ffn_casts"] is None:
            ffn_casts_global = []
            for fi in range(NT):
                op("scalar", lambda e, fi=fi: e.sign(SW[fi][:], SW[fi][:]),
                   waits=[("dsw", 1)], inc="s")
                ffn_casts_global.append(C["s"])
            st34["ffn_casts"] = ffn_casts_global
            st34["csw_v"] = csw_v_holder[0]

    def emit_b_stat(Q, i):
        w0 = [("v", st34["csw_v"]), ("s", st34["pzrow_free_s"]), ("v", ubf_v[(Q, i)])]
        wi = [("v", ubf_v[(Q, i)])]
        op("tensor", lambda e, i=i, Q=Q: e.matmul(
            PX[0:1, 0:512], ONEC[:, 0:1], XB[i][:, Q * 512:(Q + 1) * 512],
            start=(i == 0), stop=False), waits=(w0 if i == 0 else wi), inc=None)
        op("tensor", lambda e, i=i, Q=Q: e.matmul(
            PX[0:1, 0:512], CSWH[:, i:i + 1], XB[i][:, Q * 512:(Q + 1) * 512],
            start=False, stop=False), inc=None)
        op("tensor", lambda e, i=i, Q=Q: e.matmul(
            PX[0:1, 0:512], CSWL[:, i:i + 1], XB[i][:, Q * 512:(Q + 1) * 512],
            start=False, stop=(i == NT - 1)),
           inc=("t" if i == NT - 1 else None))
        if i == NT - 1:
            st34["stat2_t"] = C["t"]

    def stageB(Q):
        ffn_casts = st34["ffn_casts"]
        if Q == 0:
            for i in range(NT):
                emit_b_stat(Q, i)
        stat2_t = st34["stat2_t"]
        op("scalar", lambda e: e.copy(MROW[:], PX[0:1, 0:512]),
           waits=[("t", stat2_t)], inc="s")
        st34["pzrow_free_s"] = C["s"]
        op("vector", lambda e: e.tensor_copy(MH2[:], MROW[:]),
           waits=[("s", C["s"]), ("t", st34["m2_guard_t"])], inc="v")
        mh2_v = C["v"]
        op("vector", lambda e: e.tensor_sub(ML2[:], MROW[:], MH2[:]), inc="v")
        m2_v = C["v"]
        st34["m2chain_v"] = m2_v
        vm = {}

        def emit_var(ot):
            wv = [("v", vm[ot]), ("s", st34["var_free_s"]),
                  ("s", st34["rvb_reader_v"]), ("v", attn_done_v)]
            op("tensor", lambda e, ot=ot: e.matmul(
                PZ[0:1, 0:512], ONEC[:, 0:1], WCSQ[ot % 2][:],
                start=(ot == 0), stop=(ot == NT - 1)),
               waits=wv, inc="t")
            wcsq_reader_t[ot % 2] = C["t"]

        for ot in range(NT):
            pk = next_pw()
            pw = pw_pool[pk]
            for i in range(NT):
                op("tensor", lambda e, i=i, ot=ot, Q=Q, pw=pw: e.matmul(
                    pw, SW[i][:, ot * 128:(ot + 1) * 128], XB[i][:, Q * 512:(Q + 1) * 512],
                    start=(i == 0), stop=False),
                   waits=[("s", ffn_casts[i]), ("v", ubf_v[(Q, i)]), pw_free[pk]],
                   inc=None)
            if ot >= 1:
                emit_var(ot - 1)
            op("tensor", lambda e, pw=pw: e.matmul(
                pw, NEGM[:], MH2[:], start=False, stop=False),
               waits=[("v", mh2_v)], inc=None)
            op("tensor", lambda e, pw=pw: e.matmul(
                pw, NEGM[:], ML2[:], start=False, stop=True),
               waits=[("v", m2_v)], inc="t")
            f_t = C["t"]
            st34["m2_guard_t"] = f_t
            op("vector", lambda e, ot=ot, pw=pw: e.scalar_tensor_tensor(
                WCT[ot][:], U[ot][:], 1.0, pw, op0=ALU.mult, op1=ALU.add),
               waits=[("t", f_t), st34["rvb_g"].get(ot)], inc="v")
            wc_v[(Q, ot)] = C["v"]
            pw_free[pk] = ("v", C["v"])
            op("vector", lambda e, ot=ot: e.tensor_mul(
                WCSQ[ot % 2][:], WCT[ot][:], WCT[ot][:]),
               waits=[("t", wcsq_reader_t[ot % 2]), ("s", kres["s"])], inc="v")
            vm[ot] = C["v"]
        emit_var(NT - 1)
        var_t_q[Q] = C["t"]

    def stageC(Q):
        op("scalar", lambda e: e.activation(VROW[:], PZ[0:1, 0:512], AF.Ln,
                                            bias=EPSLC[0:1, 0:1], scale=SCL1K[0:1, 0:1]),
           waits=[("t", var_t_q[Q]), ("v", st34["rvchain_v"])], inc="s")
        st34["var_free_s"] = C["s"]
        op("scalar", lambda e: e.activation(VROW2[:], VROW[:], AF.Exp,
                                            scale=NEGH[0:1, 0:1]), inc="s")
        op("vector", lambda e: e.tensor_copy(RVH[:], VROW2[:]),
           waits=[("s", C["s"])], inc="v")
        rvh_v = C["v"]
        op("vector", lambda e: e.tensor_sub(RVL[:], VROW2[:], RVH[:]), inc="v")
        rv_v = C["v"]
        st34["rvchain_v"] = rv_v
        rvw = [("v", rvh_v), ("s", st34["rvb_reader_v"]), ("s", st34["var_free_s"])]
        if Q == 0:
            rvw.append(("v", attn_done_v))
        op("tensor", lambda e: e.matmul(PZ[:, 0:512], ONE1[:], RVH[:],
                                        start=True, stop=False),
           waits=rvw, inc=None)
        op("tensor", lambda e: e.matmul(PZ[:, 0:512], ONE1[:], RVL[:],
                                        start=False, stop=True),
           waits=[("v", rv_v)], inc="t")
        rvb_t = C["t"]
        op("scalar", lambda e: e.copy(RVB2[:], PZ[:, 0:512]),
           waits=[("t", rvb_t), ("g", st34["rvb_reader_g"]),
                  ("v", st34["rvb_reader_vm"]), ("v", attn_done_v)], inc="s")
        rsvb_s = C["s"]
        st34["rvb_reader_v"] = rsvb_s
        for ot in range(NT):
            dsem = f"do{ot % 3}"
            eng, esem = ("gpsimd", "g") if ot % 2 == 0 else ("vector", "v")
            op(eng, lambda e, ot=ot: e.tensor_mul(
                OUTS[ot % 3][:], WCT[ot][:], RVB2[:]),
               waits=[("s", rsvb_s), (dsem, 1), ("v", wc_v[(Q, ot)])], inc=esem)
            st34["rvb_g"][ot] = (esem, C[esem])
            if esem == "g":
                st34["rvb_reader_g"] = C["g"]
            else:
                st34["rvb_reader_vm"] = C["v"]
            op("sync", lambda e, ot=ot, Q=Q: e.dma_start(
                out=o_h[Q * D + ot * 128:Q * D + (ot + 1) * 128, :], in_=OUTS[ot % 3][:]),
               waits=[(esem, C[esem])], inc=dsem)

    stageA(0)
    stageB(0)
    for Q in range(1, CH):
        stageA(Q, cprev=Q - 1)
        stageB(Q)
    stageC(CH - 1)

    op("sync", lambda e: e.nop(),
       waits=[(s, C[s]) for s in DMA_SEMS if C[s] > 0])

    B.check()
    B.emit()
    return nc


_NC = None
_CONSTS = None


def kernel(L0_attn_x, L0_attn_Wq, L0_attn_bq, L0_attn_Wk, L0_attn_bk,
           L0_attn_Wv, L0_attn_bv, L0_attn_Wo, L0_attn_norm_w, L0_ffn_weight):
    global _NC, _CONSTS
    if _NC is None:
        _NC = build_nc()
    if _CONSTS is None:
        _CONSTS = _consts()
    x = np.asarray(L0_attn_x, dtype=np.float32)
    nw = np.asarray(L0_attn_norm_w, dtype=np.float32)
    idx = np.arange(128) % 64
    shared = dict(
        wqt=np.ascontiguousarray(np.asarray(L0_attn_Wq, np.float32).T),
        wkt=np.ascontiguousarray(np.asarray(L0_attn_Wk, np.float32).T),
        wvt=np.ascontiguousarray(np.asarray(L0_attn_Wv, np.float32).T),
        wot=np.ascontiguousarray(np.asarray(L0_attn_Wo, np.float32).T),
        wft=np.ascontiguousarray(np.asarray(L0_ffn_weight, np.float32).T),
        bq=np.ascontiguousarray(np.asarray(L0_attn_bq, np.float32)),
        bk=np.ascontiguousarray(np.asarray(L0_attn_bk, np.float32)),
        bv=np.ascontiguousarray(np.asarray(L0_attn_bv, np.float32)),
        nw2=np.ascontiguousarray(nw[idx]),
        nwp2=np.ascontiguousarray(nw[idx ^ 1]),
        **_CONSTS,
    )
    in_maps = []
    for c in range(8):
        m = dict(shared)
        m["xt"] = np.ascontiguousarray(x[c].T)
        in_maps.append(m)
    res = run_bass_kernel_spmd(_NC, in_maps, core_ids=list(range(8)))
    def reasm(a):
        a = np.asarray(a).reshape(CH, D, 512)
        return np.concatenate(list(a), axis=1).T
    out = np.stack([np.ascontiguousarray(reasm(r["o"])) for r in res.results], axis=0)
    if _TAPS:
        kernel._last_taps = res.results
    return out.astype(np.float32)


# version 31 (final)
# speedup vs baseline: 1.0342x; 1.0342x over previous
import numpy as np
import ml_dtypes
import concourse.bass as bass
import concourse.mybir as mybir
import concourse.dve_ops as dve_ops
from concourse.bass_utils import run_bass_kernel_spmd
from contextlib import ExitStack

F32 = mybir.dt.float32
BF16 = mybir.dt.bfloat16
AF = mybir.ActivationFunctionType
ALU = mybir.AluOpType

S = 2048
D = 1024
NT = 8
STL = 16
CH = 4
DH = 64
NHP = 8
EPS_RMS = float(np.finfo(np.float32).eps)
EPS_LN = 1e-5
KB = 1024

ENGS = ("sync", "tensor", "scalar", "vector", "gpsimd")
DMA_SEMS = {"dc", "dt", "dx0", "dx1", "dxc", "dxc1", "dxc2", "dxc3", "dout", "do0", "do1", "do2",
            "dmag0", "dmag1", "df0", "df1", "dq0", "dq1", "dh", "dv", "dO", "dsw"}

_TAPS = False


def _consts():
    invf = (10000.0 ** (-(np.arange(0, DH, 2) / DH))).astype(np.float64)
    f = invf[np.arange(128) % 32]
    t = np.arange(S, dtype=np.float64)
    ang = f[:, None] * t[None, :]
    cost = np.cos(ang).astype(np.float32)
    sint = np.sin(ang).astype(np.float32)
    P = np.zeros((128, 128), np.float32)
    for i in range(64):
        P[2 * i, 2 * i + 1] = -1.0
        P[2 * i + 1, 2 * i] = 1.0
    pt = np.ascontiguousarray(P.T).astype(np.float32)
    tri1 = (np.arange(128)[None, :] >= np.arange(128)[:, None]).astype(np.float32)
    tri = np.concatenate([tri1, tri1], axis=1)
    emat = (np.arange(D)[None, :] // DH == np.arange(16)[:, None]).astype(np.float32)
    hpad = np.zeros((128, 31), np.float32)
    for p in range(128):
        hpad[p, 15 + (p >= 64)] = 1.0
    spad = np.zeros((16, 1152), np.float32)
    for h in range(16):
        base = (7 - h // 2) * 128 + 64 * (h % 2)
        spad[h, base:base + 64] = 1.0
    return dict(cost=cost, sint=sint, pt=pt, tri=tri, emat=emat, hpad=hpad, spad=spad)


class Builder:
    SERIAL = {"scalar": "s", "vector": "v", "gpsimd": "g"}

    def __init__(self, nc):
        self.nc = nc
        self.ops = {e: [] for e in ENGS}
        self.cnt = {}
        self.lastw = {e: {} for e in ENGS}
        self.sems = {}

    def sem(self, name, ctx):
        self.sems[name] = ctx.enter_context(self.nc.semaphore(name))
        self.cnt[name] = 0

    def op(self, eng, fn, waits=(), inc=None):
        own = self.SERIAL.get(eng)
        if own is not None and inc is None:
            inc = own
        if own is not None and self.cnt[own] > 0:
            waits = list(waits) + [(own, self.cnt[own])]
        wl = []
        for item in waits:
            if item is None:
                continue
            sname, v = item
            if v is None or v <= 0:
                continue
            if sname in DMA_SEMS:
                v = self.cnt[sname]
            if v > self.lastw[eng].get(sname, 0):
                self.lastw[eng][sname] = v
                wl.append((sname, v))
        amt = 0
        tot = None
        if inc is not None:
            amt = 16 if inc in DMA_SEMS else 1
            self.cnt[inc] += amt
            tot = self.cnt[inc]
        self.ops[eng].append((fn, tuple(wl), inc, amt))
        return tot

    def check(self):
        ptr = {e: 0 for e in ENGS}
        val = {s: 0 for s in self.cnt}
        total = sum(len(v) for v in self.ops.values())
        done = 0
        while done < total:
            prog = False
            for e in ENGS:
                while ptr[e] < len(self.ops[e]):
                    _, wl, inc, amt = self.ops[e][ptr[e]]
                    if all(val[s] >= v for s, v in wl):
                        if inc:
                            val[inc] += amt
                        ptr[e] += 1
                        done += 1
                        prog = True
                    else:
                        break
            if not prog:
                detail = {e: (ptr[e], len(self.ops[e]),
                              self.ops[e][ptr[e]][1] if ptr[e] < len(self.ops[e]) else None)
                          for e in ENGS}
                raise RuntimeError(f"deadlock: sems={val} at (ptr,len,waits)={detail}")

    def emit(self):
        sems = self.sems
        with self.nc.Block() as block:
            def runner(eng_ops):
                def body(e):
                    for fn, wl, inc, amt in eng_ops:
                        for sname, v in wl:
                            e.wait_ge(sems[sname], v)
                        r = fn(e)
                        if inc is not None:
                            r.then_inc(sems[inc], amt)
                return body
            block.sync(runner(self.ops["sync"]))
            block.tensor(runner(self.ops["tensor"]))
            block.scalar(runner(self.ops["scalar"]))
            block.vector(runner(self.ops["vector"]))
            block.gpsimd(runner(self.ops["gpsimd"]))


def build_nc():
    nc = bass.Bass("TRN2", target_bir_lowering=False)
    ctx = ExitStack()

    xt_h = nc.dram_tensor("xt", [D, S], F32, kind="ExternalInput").ap()
    w_h = {w: nc.dram_tensor(w, [D, D], F32, kind="ExternalInput").ap()
           for w in ("wqt", "wkt", "wvt", "wot", "wft")}
    b_h = {b: nc.dram_tensor(b, [D], F32, kind="ExternalInput").ap()
           for b in ("bq", "bk", "bv")}
    nw2_h = nc.dram_tensor("nw2", [128], F32, kind="ExternalInput").ap()
    nwp2_h = nc.dram_tensor("nwp2", [128], F32, kind="ExternalInput").ap()
    cost_h = nc.dram_tensor("cost", [128, S], F32, kind="ExternalInput").ap()
    sint_h = nc.dram_tensor("sint", [128, S], F32, kind="ExternalInput").ap()
    pt_h = nc.dram_tensor("pt", [128, 128], F32, kind="ExternalInput").ap()
    tri_h = nc.dram_tensor("tri", [128, 256], F32, kind="ExternalInput").ap()
    emat_h = nc.dram_tensor("emat", [16, D], F32, kind="ExternalInput").ap()
    hpad_h = nc.dram_tensor("hpad", [128, 31], F32, kind="ExternalInput").ap()
    spad_h = nc.dram_tensor("spad", [16, 1152], F32, kind="ExternalInput").ap()
    o_h = nc.dram_tensor("o", [CH * D, 512], F32, kind="ExternalOutput").ap()
    taps = {}
    if _TAPS:
        for nm, shp, dt in (("tap_q", [D, S], BF16), ("tap_k", [D, S], BF16),
                            ("tap_v", [S, D], BF16), ("tap_z", [D, S], BF16),
                            ("tap_u", [D, S], BF16), ("tap_rq", [16, S], BF16),
                            ("tap_rk", [16, S], BF16), ("tap_sum2x", [1, S], BF16)):
            taps[nm] = nc.dram_tensor(nm, shp, dt, kind="ExternalOutput").ap()

    SBUF_BASE = 16512
    def at(name, shape, dtype, off):
        end = off + SBUF_BASE + int(np.prod(shape[1:])) * (2 if dtype == BF16 else 4)
        assert end <= 229344, (name, end)
        return nc.alloc_sbuf_tensor_at(name, shape, dtype, offset=off + SBUF_BASE)

    XB = [at(f"xb{i}", [128, S], BF16, i * 4 * KB) for i in range(NT)]
    QT = [at(f"qt{i}", [128, S], BF16, 32 * KB + i * 4 * KB) for i in range(NT)]
    KT = [at(f"kt{i}", [128, S], BF16, 64 * KB + i * 4 * KB) for i in range(NT)]
    U = [at(f"u{i}", [128, 512], F32, 64 * KB + i * 2 * KB) for i in range(NT)]
    RO = 80 * KB
    MROW = at("mrow", [1, 512], F32, RO)
    MROW2 = at("mrow2", [1, 512], F32, RO + 2 * KB)
    MH = at("mh", [1, 512], BF16, RO + 4 * KB)
    MH2 = at("mh2", [1, 512], BF16, RO + 5 * KB)
    ML2 = at("ml2", [1, 512], BF16, RO + 6 * KB)
    ML = at("ml", [1, 512], BF16, RO + 7 * KB)
    VROW = at("vrow", [1, 512], F32, RO + 8 * KB)
    VROW2 = at("vrow2", [1, 512], F32, RO + 10 * KB)
    RVH = at("rvh", [1, 512], BF16, RO + 12 * KB)
    RVL = at("rvl", [1, 512], BF16, RO + 15 * KB)
    VT = [at(f"vt{i}", [128, D], BF16, 96 * KB + i * 2 * KB) for i in range(STL)]
    QRAW = at("qraw", [128, S], F32, 96 * KB)
    QRB = [at(f"qrb{i}", [128, 512], BF16, 104 * KB + i * KB) for i in range(4)]
    QSQ = [at(f"qsq{i}", [128, 512], BF16, 108 * KB + i * KB) for i in range(4)]
    T1 = [at(f"t1{i}", [128, 512], F32, 112 * KB + i * 2 * KB) for i in range(4)]
    T2 = [at(f"t2{i}", [128, 512], F32, 120 * KB + i * 2 * KB) for i in range(4)]
    SW = [at(f"sw{i}", [128, D], BF16, 96 * KB + i * 2 * KB) for i in range(NT)]
    WCT = [at(f"wct{i}", [128, 512], F32, 112 * KB + i * 2 * KB) for i in range(NT)]
    RQH = at("rqh", [16, S], BF16, 128 * KB)
    RQL = at("rql", [16, S], BF16, 132 * KB)
    RKH = at("rkh", [16, S], BF16, 136 * KB)
    RKL = at("rkl", [16, S], BF16, 140 * KB)
    SC1 = at("sc1", [16, 512], F32, 144 * KB)
    SC2 = at("sc2", [16, 512], F32, 146 * KB)
    PTC = at("ptc", [128, 128], BF16, 148 * KB)
    TRIC = at("tric", [128, 256], BF16, 148 * KB + 256)
    EMC = at("emc", [16, D], BF16, 148 * KB + 832)
    HPC = at("hpc", [128, 31], BF16, 148 * KB + 768)
    SPC = at("spc", [16, 1152], BF16, 150 * KB + 896)
    ONE1 = at("one1", [1, 128], BF16, 153 * KB + 256)
    ONEC = at("onec", [128, 1], BF16, 153 * KB + 512)
    TWO1 = at("two1", [128, 1], BF16, 153 * KB + 544)
    NEGM = at("negm", [1, 128], BF16, 153 * KB + 576)
    BQC = at("bqc", [128, NT], F32, 153 * KB + 832)
    BKC = at("bkc", [128, NT], F32, 153 * KB + 864)
    NWC = at("nwc", [128, 1], F32, 153 * KB + 896)
    NWPC = at("nwpc", [128, 1], F32, 153 * KB + 928)
    CSWOH = at("cswoh", [128, NT], BF16, 153 * KB + 960)
    CSWOL = at("cswol", [128, NT], BF16, 153 * KB + 992)
    CSWH = at("cswh", [128, NT], BF16, 154 * KB)
    CSWL = at("cswl", [128, NT], BF16, 154 * KB + 32)
    CSWO = at("cswo", [128, NT], F32, 154 * KB + 64)
    CSWOF = at("cswof", [128, NT], F32, 154 * KB + 96)
    EPSRC = at("epsrc", [16, 1], F32, 154 * KB + 128)
    SCL64 = at("scl64", [16, 1], F32, 154 * KB + 160)
    EPSK = at("epsk", [16, 1], F32, 154 * KB + 192)
    SCLK = at("sclk", [16, 1], F32, 154 * KB + 224)
    EPSLC = at("epslc", [1, 1], F32, 154 * KB + 256)
    SCL1K = at("scl1k", [1, 1], F32, 154 * KB + 288)
    NEG1C = at("neg1c", [16, 1], F32, 154 * KB + 320)
    NEGH = at("negh", [1, 1], F32, 154 * KB + 352)
    SUM2X = at("sum2x", [1, S], BF16, 156 * KB)
    XSTG = [at(f"xstg{i}", [128, S], F32, 160 * KB + i * 8 * KB) for i in range(2)]
    COST = at("costb", [128, S], F32, 160 * KB)
    SINT = at("sintb", [128, S], F32, 168 * KB)
    PATF = [at(f"patf{i}", [128, 1024], BF16, 160 * KB + i * 2 * KB) for i in range(6)]
    RSQB = [at(f"rsqb{i}", [128, 512], F32, 172 * KB + i * 2 * KB) for i in range(2)]
    XC = [at(f"xc{i}", [128, 512], F32, 160 * KB + i * 2 * KB) for i in range(NT)]
    WB = [at(f"wb{i}", [128, D], BF16, 176 * KB + i * 2 * KB) for i in range(NT)]
    WS = [at(f"ws{i}", [128, D], F32, 192 * KB + i * 4 * KB) for i in range(2)]
    WH = [at(f"wh{i}", [128, D], BF16, 192 * KB + i * 2 * KB) for i in range(7)]
    WS2 = at("ws2", [128, D], F32, 200 * KB)
    SGNT = at("sgnt", [128, D], BF16, 204 * KB)
    RVB2 = at("rvb2", [128, 512], F32, 192 * KB)
    SPST = at("spst", [16, 1152], F32, 200 * KB)
    BVB = at("bvb", [128, D], F32, 200 * KB)
    OUTS = [at(f"outs{i}", [128, 512], F32, 200 * KB + i * 2 * KB) for i in range(3)]
    VTMP = at("vtmp", [128, 512], F32, 204 * KB)
    WCSQ = [at(f"wcsq{i}", [128, 512], BF16, 144 * KB + i * KB) for i in range(2)]
    BVROW = at("bvrow", [1, D], F32, 96 * KB)
    BVH = at("bvh", [1, D], BF16, 100 * KB)
    BVL = at("bvl", [1, D], BF16, 102 * KB)
    BVHF = at("bvhf", [1, D], F32, 104 * KB)

    PAIR = [ctx.enter_context(nc.psum_tensor(f"pp{i}", [128, 1024], F32)) for i in range(3)]
    PZ = ctx.enter_context(nc.psum_tensor("pz", [128, 512], F32))
    PX = ctx.enter_context(nc.psum_tensor("px", [128, 512], F32))

    B = Builder(nc)
    for s in ("dc", "dt", "dx0", "dx1", "dxc", "dxc1", "dxc2", "dxc3", "dout",
              "do0", "do1", "do2", "dmag0", "dmag1", "df0", "df1", "dq0", "dq1",
              "dh", "dv", "dO", "dsw", "t", "s", "v", "g"):
        B.sem(s, ctx)
    op = B.op
    C = B.cnt

    op("sync", lambda e: e.dma_start(out=WS[0][:, 0:128], in_=pt_h), inc="dc")
    op("sync", lambda e: e.dma_start(out=WS[0][:, 128:384], in_=tri_h), inc="dc")
    op("sync", lambda e: e.dma_start(out=WS[0][:, 384:415], in_=hpad_h), inc="dc")
    op("sync", lambda e: e.dma_start(out=WS[1][0:16, :], in_=emat_h), inc="dc")
    op("sync", lambda e: e.dma_start(out=SPST[:], in_=spad_h), inc="dc")
    for src, dst in ((nw2_h, NWC), (nwp2_h, NWPC)):
        op("sync", lambda e, s=src, d=dst: e.dma_start(out=d[:], in_=s), inc="dc")
    for t in range(NT):
        op("sync", lambda e, t=t: e.dma_start(out=BQC[:, t:t + 1], in_=b_h["bq"][t * 128:(t + 1) * 128]), inc="dc")
        op("sync", lambda e, t=t: e.dma_start(out=BKC[:, t:t + 1], in_=b_h["bk"][t * 128:(t + 1) * 128]), inc="dc")
    consts_done = C["dc"]
    op("gpsimd", lambda e: e.tensor_copy(PTC[:], WS[0][:, 0:128]),
       waits=[("dc", consts_done)], inc="g")
    op("gpsimd", lambda e: e.tensor_copy(TRIC[:], WS[0][:, 128:384]), inc="g")
    op("gpsimd", lambda e: e.tensor_copy(HPC[:], WS[0][:, 384:415]), inc="g")
    op("gpsimd", lambda e: e.tensor_copy(EMC[:], WS[1][0:16, :]), inc="g")
    op("gpsimd", lambda e: e.tensor_copy(SPC[:], SPST[:]), inc="g")
    cconst_g = C["g"]
    op("gpsimd", lambda e: e.memset(ONE1[:], 1.0), inc="g")
    op("gpsimd", lambda e: e.memset(ONEC[:], 1.0), inc="g")
    op("gpsimd", lambda e: e.memset(TWO1[:], 2.0), inc="g")
    op("gpsimd", lambda e: e.memset(NEGM[:], -1.0 / 1024.0), inc="g")
    op("gpsimd", lambda e: e.memset(EPSRC[:], EPS_RMS), inc="g")
    op("gpsimd", lambda e: e.memset(SCL64[:], 1.0 / 64.0), inc="g")
    op("gpsimd", lambda e: e.memset(EPSK[:], 4096.0 * EPS_RMS), inc="g")
    op("gpsimd", lambda e: e.memset(SCLK[:], 64.0), inc="g")
    op("gpsimd", lambda e: e.memset(EPSLC[:], EPS_LN), inc="g")
    op("gpsimd", lambda e: e.memset(SCL1K[:], 1.0 / 1024.0), inc="g")
    op("gpsimd", lambda e: e.memset(NEG1C[:], -1.0), inc="g")
    op("gpsimd", lambda e: e.memset(NEGH[:], -0.5), inc="g")
    memset_done = C["g"]

    xcast = []
    for i in range(NT):
        stg = XSTG[i % 2]
        w = [("v", xcast[i - 2])] if i >= 2 else []
        op("sync",
           lambda e, i=i, stg=stg: e.dma_start(out=stg[:], in_=xt_h[i * 128:(i + 1) * 128, :]),
           waits=w, inc=("dx0" if i % 2 == 0 else "dx1"))
        op("vector", lambda e, i=i, stg=stg: e.tensor_copy(XB[i][:], stg[:]),
           waits=[("dx0" if i % 2 == 0 else "dx1", 1)], inc="v")
        xcast.append(C["v"])
    xcast_done = C["v"]
    op("sync", lambda e: e.dma_start(out=COST[:], in_=cost_h), waits=[("v", xcast_done)], inc="dt")
    op("sync", lambda e: e.dma_start(out=SINT[:], in_=sint_h), inc="dt")
    tables_loaded = C["dt"]
    op("vector", lambda e: e.tensor_scalar_mul(COST[:], COST[:], NWC[:, 0:1]),
       waits=[("dt", tables_loaded), ("dc", consts_done)], inc="v")
    op("vector", lambda e: e.tensor_scalar_mul(SINT[:], SINT[:], NWPC[:, 0:1]), inc="v")
    tables_done = C["v"]

    def stat_ps(c):
        return PAIR[c // 2][0:1, (c % 2) * 512:(c % 2) * 512 + 512]
    for c in range(CH):
        for i in range(NT):
            op("tensor", lambda e, c=c, i=i: e.matmul(
                stat_ps(c), TWO1[:, 0:1], XB[i][:, c * 512:(c + 1) * 512],
                start=(i == 0), stop=(i == NT - 1)),
               waits=[("v", xcast[i]), ("g", memset_done)],
               inc=("t" if i == NT - 1 else None))
        op("scalar", lambda e, c=c: e.copy(SUM2X[0:1, c * 512:(c + 1) * 512], stat_ps(c)),
           waits=[("t", C["t"])], inc="s")
    xstats_done_s = C["s"]

    def wdma(name, i, dst, dsem, waits=()):
        op("gpsimd", lambda e, name=name, i=i, dst=dst: e.dma_start(
            out=dst[:], in_=w_h[name][i * 128:(i + 1) * 128, :]),
           waits=waits, inc=dsem)

    for i in range(NT):
        wdma("wqt", i, WB[i], "dq0" if i < 4 else "dq1",
             waits=[("g", memset_done)] if i == 0 else ())

    def qk_phase(which, prev):
        dst = QT if which == 0 else KT
        bias = BQC if which == 0 else BKC
        rhi = RQH if which == 0 else RKH
        rlo = RQL if which == 0 else RKL
        epsc, sclc = (EPSRC, SCL64) if which == 0 else (EPSK, SCLK)
        if which == 1:
            wdma("wkt", 7, WB[7], "dq0", waits=[("t", prev["last_proj_t"])])
        NID = NT * CH
        mm_t, evac_s, sq_s, cast_v, tt1_v, tt2_v, tt3_g, rotmm, ssq_t = ({} for _ in range(9))
        sumsq_stop_t = [0] * CH
        epi_state = [prev["sc_free_v"]]

        def emit_epilogue(c):
            ps = PAIR[c // 2][0:16, (c % 2) * 512:(c % 2) * 512 + 512]
            op("scalar", lambda e, ps=ps: e.activation(
                SC1[:], ps, AF.Ln, bias=epsc[:, 0:1], scale=sclc[:, 0:1]),
               waits=[("t", sumsq_stop_t[c]), ("g", memset_done),
                      ("v", epi_state[0])], inc="s")
            op("scalar", lambda e: e.activation(SC2[:], SC1[:], AF.Exp, scale=NEG1C[:, 0:1]), inc="s")
            op("vector", lambda e, c=c: e.tensor_copy(
                rhi[:, c * 512:(c + 1) * 512], SC2[:]),
               waits=[("s", C["s"])], inc="v")
            op("vector", lambda e, c=c: e.tensor_copy(
                SC1[:], rhi[:, c * 512:(c + 1) * 512]), inc="v")
            op("vector", lambda e, c=c: e.tensor_sub(
                rlo[:, c * 512:(c + 1) * 512], SC2[:], SC1[:]), inc="v")
            epi_state[0] = C["v"]

        def emit_trailing(j):
            c, ot = divmod(j, NT)
            pr = (PZ[:, 0:512], PX[:, 0:512])[j % 2]
            op("tensor", lambda e, j=j, pr=pr: e.matmul(
                pr, PTC[:], QRB[j % 4][:], start=True, stop=True),
               waits=[("s", cast_v[j]), ("g", cconst_g),
                      ("v", tt2_v.get(j - 2, prev["v"] if j < 2 else 0))], inc="t")
            rotmm[j] = C["t"]
            ssw = [("s", sq_s[j])]
            if ot == 0:
                ssw.append(("s", prev["pair_free_s"]))
                ssw.append(("g", cconst_g))
            op("tensor", lambda e, j=j, ot=ot, c=c: e.matmul(
                PAIR[c // 2][0:16, (c % 2) * 512:(c % 2) * 512 + 512],
                HPC[:, 15 - 2 * ot:31 - 2 * ot], QSQ[j % 4][:],
                start=(ot == 0), stop=(ot == NT - 1)),
               waits=ssw, inc="t")
            ssq_t[j] = C["t"]
            if ot == NT - 1:
                sumsq_stop_t[c] = C["t"]
            op("vector", lambda e, j=j, c=c, pr=pr: e.tensor_mul(
                T2[j % 4][:], pr, SINT[:, c * 512:(c + 1) * 512]),
               waits=[("t", rotmm[j]),
                      ("g", tt3_g.get(j - 4, prev["g"] if j < 4 else 0))], inc="v")
            tt2_v[j] = C["v"]
            op("gpsimd", lambda e, j=j, ot=ot, c=c, dst=dst: e.tensor_add(
                dst[ot][:, c * 512:(c + 1) * 512], T1[j % 4][:], T2[j % 4][:]),
               waits=[("v", tt2_v[j]), ("v", tt1_v[j])], inc="g")
            tt3_g[j] = C["g"]

        for idx in range(NID):
            c, ot = divmod(idx, NT)
            pp = PAIR[2][:, (idx % 2) * 512:(idx % 2) * 512 + 512]
            w0 = [("s", evac_s.get(idx - 2, prev["pp_free_s"][idx % 2]))]
            for i in range(NT):
                wt = WB[i] if (which == 0 or i == 7) else WH[i]
                if which == 0:
                    dws = (("dq0" if i < 4 else "dq1"), 1)
                else:
                    dws = (("dq0" if i == 7 else "dh"), 1)
                op("tensor", lambda e, i=i, ot=ot, c=c, pp=pp, wt=wt: e.matmul(
                    pp, wt[:, ot * 128:(ot + 1) * 128],
                    XB[i][:, c * 512:(c + 1) * 512],
                    start=(i == 0), stop=(i == NT - 1)),
                   waits=([*w0, dws] if i == 0 else [dws]),
                   inc=("t" if i == NT - 1 else None))
            mm_t[idx] = C["t"]
            sl = (idx % 4) * 512
            op("scalar", lambda e, sl=sl, ot=ot, pp=pp, bias=bias: e.activation(
                QRAW[:, sl:sl + 512], pp, AF.Identity,
                bias=bias[:, ot:ot + 1], scale=1.0),
               waits=[("t", mm_t[idx]), ("dc", consts_done),
                      ("v", tt1_v.get(idx - 4, prev["v"] if idx < 4 else 0))], inc="s")
            evac_s[idx] = C["s"]
            op("scalar", lambda e, idx=idx, sl=sl: e.square(
                QSQ[idx % 4][:], QRAW[:, sl:sl + 512]),
               waits=[("t", ssq_t.get(idx - 4, prev["t"] if idx < 4 else 0))], inc="s")
            sq_s[idx] = C["s"]
            op("scalar", lambda e, idx=idx, sl=sl: e.copy(
                QRB[idx % 4][:], QRAW[:, sl:sl + 512]),
               waits=[("t", rotmm.get(idx - 4, prev["t"] if idx < 4 else 0))], inc="s")
            cast_v[idx] = C["s"]
            op("vector", lambda e, idx=idx, sl=sl, c=c: e.tensor_mul(
                T1[idx % 4][:], QRAW[:, sl:sl + 512],
                COST[:, c * 512:(c + 1) * 512]),
               waits=[("v", tables_done), ("s", evac_s[idx]),
                      ("g", tt3_g.get(idx - 4, prev["g"] if idx < 4 else 0))], inc="v")
            tt1_v[idx] = C["v"]
            if idx >= 2:
                emit_trailing(idx - 2)
                if (idx - 2) % NT == NT - 1 and (idx - 2) // NT < CH - 1:
                    emit_epilogue((idx - 2) // NT)
            if idx == 2:
                if which == 0:
                    for i in range(7):
                        wdma("wkt", i, WH[i], "dh")
                else:
                    for i in range(7):
                        wdma("wvt", i, WB[i], "dv",
                             waits=[("t", prev["last_proj_t"])] if i == 0 else ())
        emit_trailing(NID - 2)
        emit_trailing(NID - 1)
        emit_epilogue(CH - 1)
        return dict(v=C["v"], t=C["t"], s=C["s"], g=C["g"],
                    pp_free_s=(evac_s[NID - 2], evac_s[NID - 1]),
                    pair_free_s=C["s"], sc_free_v=epi_state[0],
                    last_proj_t=mm_t[NID - 1],
                    tile_g={tt: tt3_g[(CH - 1) * NT + tt] for tt in range(NT)})

    prev0 = dict(t=0, v=0, s=0, g=memset_done, pp_free_s=(0, 0),
                 pair_free_s=xstats_done_s, sc_free_v=0)
    qres = qk_phase(0, prev0)
    kres = qk_phase(1, qres)

    wdma("wvt", 7, WB[7], "dq1", waits=[("t", kres["last_proj_t"])])
    op("sync", lambda e: e.dma_start(out=BVROW[:], in_=b_h["bv"][:]),
       waits=[("v", kres["v"])], inc="dt")
    op("vector", lambda e: e.tensor_copy(BVH[:], BVROW[:]), waits=[("dt", 1)], inc="v")
    op("vector", lambda e: e.tensor_sub(BVL[:], BVROW[:], BVH[:]), inc="v")
    bv_v = C["v"]
    vmm_t, rmm_t, vtta_v, vttb_v = {}, {}, {}, {}
    vpool = [PAIR[2][:, 0:512], PAIR[2][:, 512:1024], PZ[:, 0:512], PX[:, 0:512]]

    def emit_v_trailing(st):
        for half in range(2):
            pr = PAIR[half][:, 0:512]
            w = [("v", kres["v"]), ("g", cconst_g),
                 ("v", vttb_v.get((st - 1, half), 0))]
            if st == 0:
                w.append(("s", bvb_s))
            op("tensor", lambda e, st=st, half=half, pr=pr: e.matmul(
                pr, RKH[:, st * 128:(st + 1) * 128],
                EMC[:, half * 512:half * 512 + 512], start=True, stop=False),
               waits=w, inc=None)
            op("tensor", lambda e, st=st, half=half, pr=pr: e.matmul(
                pr, RKL[:, st * 128:(st + 1) * 128],
                EMC[:, half * 512:half * 512 + 512], start=False, stop=True), inc="t")
            rmm_t[(st, half)] = C["t"]
            pp = vpool[(2 * st + half) % 4]
            op("vector", lambda e, st=st, half=half, pp=pp: e.tensor_add(
                VTMP[:], pp, BVB[:, half * 512:half * 512 + 512]),
               waits=[("t", vmm_t[(st, half)]), ("s", bvb_s)], inc="v")
            vtta_v[(st, half)] = C["v"]
            op("vector", lambda e, st=st, half=half, pr=pr: e.tensor_mul(
                VT[st][:, half * 512:half * 512 + 512], VTMP[:], pr),
               waits=[("t", rmm_t[(st, half)])], inc="v")
            vttb_v[(st, half)] = C["v"]

    bvb_emitted = [False]

    def emit_bvb():
        for half in range(2):
            op("tensor", lambda e, half=half: e.matmul(
                PAIR[half][:, 512:1024], ONE1[:],
                BVH[0:1, half * 512:half * 512 + 512], start=True, stop=False),
               waits=[("v", bv_v), ("s", kres["s"])], inc=None)
            op("tensor", lambda e, half=half: e.matmul(
                PAIR[half][:, 512:1024], ONE1[:],
                BVL[0:1, half * 512:half * 512 + 512], start=False, stop=True), inc="t")
        bvmm = C["t"]
        op("scalar", lambda e: e.copy(BVB[:, 0:512], PAIR[0][:, 512:1024]),
           waits=[("t", bvmm)], inc="s")
        op("scalar", lambda e: e.copy(BVB[:, 512:1024], PAIR[1][:, 512:1024]), inc="s")
        return C["s"]

    for st in range(STL):
        if st >= 2:
            emit_v_trailing(st - 2)
        for half in range(2):
            pp = vpool[(2 * st + half) % 4]
            if st < 2:
                if (2 * st + half) % 4 < 2:
                    w2 = [("s", kres["pp_free_s"][(2 * st + half) % 4])]
                else:
                    w2 = [("v", kres["v"])]
            else:
                w2 = [("v", vtta_v[(st - 2, half)])]
            for i in range(NT):
                op("tensor", lambda e, st=st, half=half, i=i, pp=pp: e.matmul(
                    pp, XB[i][:, st * 128:(st + 1) * 128],
                    WB[i][:, half * 512:half * 512 + 512],
                    start=(i == 0), stop=(i == NT - 1)),
                   waits=([*w2, ("dq1" if i == 7 else "dv", 1)] if i == 0
                          else [("dq1" if i == 7 else "dv", 1)]),
                   inc=("t" if i == NT - 1 else None))
            vmm_t[(st, half)] = C["t"]
        if st == 0:
            bvb_s = emit_bvb()
    emit_v_trailing(STL - 2)
    emit_v_trailing(STL - 1)
    last_vproj_t = vmm_t[(STL - 1, 1)]
    v_done_v = C["v"]
    v_done_t = C["t"]

    if _TAPS:
        op("sync", lambda e: e.dma_start(out=taps["tap_rq"], in_=RQH[:]),
           waits=[("v", v_done_v)], inc="dout")
        op("sync", lambda e: e.dma_start(out=taps["tap_rk"], in_=RKH[:]), inc="dout")
        op("sync", lambda e: e.dma_start(out=taps["tap_sum2x"], in_=SUM2X[:]), inc="dout")
        for i in range(NT):
            op("sync", lambda e, i=i: e.dma_start(out=taps["tap_q"][i * 128:(i + 1) * 128, :], in_=QT[i][:]),
               waits=[("v", v_done_v), ("g", kres["tile_g"][i])], inc="dout")
            op("sync", lambda e, i=i: e.dma_start(out=taps["tap_k"][i * 128:(i + 1) * 128, :], in_=KT[i][:]),
               inc="dout")
        for i in range(STL):
            op("sync", lambda e, i=i: e.dma_start(out=taps["tap_v"][i * 128:(i + 1) * 128, :], in_=VT[i][:]),
               inc="dout")
    taps_qkv_dma = C["dout"]

    for i in range(NT):
        wdma("wot", i, WB[i], "dO",
             waits=[("t", last_vproj_t)] if i == 0 else ())
    cswo_v = None

    pat_ready, pat_map, pat_slot, pat_evac = {}, {}, {}, {}
    patf_user = [("g", 0)] * 6
    pair_user = [("v", v_done_v)] * 3
    jj = [0]
    zevac_v = {}
    rsqb_s = [0, 0]
    rsqb_reader = [0, 0]
    items = [(Q, t, j) for Q in range(CH) for t in range(NHP) for j in range(4 * (Q + 1))]

    def emit_scores(it):
        Q, t, j = it
        r = j * 128 - Q * 512
        off = max(0, r)
        pair = PAIR[jj[0] % 3]
        w = [("g", kres["tile_g"][t])]
        prev3 = items[jj[0] - 3] if jj[0] >= 3 else None
        if prev3 is not None:
            w.append(pat_evac[prev3])
        else:
            w.append(pair_user[jj[0] % 3])
        q0, q1 = Q * 512 + off, (Q + 1) * 512
        k0, k1 = j * 128, (j + 1) * 128
        op("tensor", lambda e, t=t, off=off, pair=pair, q0=q0, q1=q1, k0=k0, k1=k1: e.matmul(
            pair[:, off:512], KT[t][0:64, k0:k1], QT[t][0:64, q0:q1],
            start=True, stop=True, tile_position=(0, 0)), waits=w, inc=None)
        op("tensor", lambda e, t=t, off=off, pair=pair, q0=q0, q1=q1, k0=k0, k1=k1: e.matmul(
            pair[:, 512 + off:1024], KT[t][64:128, k0:k1], QT[t][64:128, q0:q1],
            start=True, stop=True, tile_position=(64, 0)), inc="t")
        sc_t = C["t"]
        pj = jj[0] % 6
        buf = PATF[pj]
        pat_map[it] = buf
        pat_slot[it] = pj
        bw = [patf_user[pj], ("v", kres["v"])]
        if jj[0] % 8 in (1, 2, 4, 6, 7):
            if off == 0:
                op("scalar", lambda e, buf=buf, pair=pair: e.activation(
                    buf[:], pair[:], AF.Square), waits=[("t", sc_t), *bw], inc="s")
            else:
                op("scalar", lambda e, buf=buf, pair=pair, off=off: e.activation(
                    buf[:, off:512], pair[:, off:512], AF.Square),
                   waits=[("t", sc_t), *bw], inc=None)
                op("scalar", lambda e, buf=buf, pair=pair, off=off: e.activation(
                    buf[:, 512 + off:1024], pair[:, 512 + off:1024], AF.Square), inc="s")
            ready = ("s", C["s"])
        else:
            if off == 0:
                op("vector", lambda e, buf=buf, pair=pair: e.tensor_copy(buf[:], pair[:]),
                   waits=[("t", sc_t), *bw], inc=None)
                op("vector", lambda e, buf=buf: e.tensor_mul(buf[:], buf[:], buf[:]), inc="v")
            else:
                op("vector", lambda e, buf=buf, pair=pair, off=off: e.tensor_copy(
                    buf[:, off:512], pair[:, off:512]), waits=[("t", sc_t), *bw], inc=None)
                op("vector", lambda e, buf=buf, pair=pair, off=off: e.tensor_copy(
                    buf[:, 512 + off:1024], pair[:, 512 + off:1024]), inc=None)
                op("vector", lambda e, buf=buf, off=off: e.tensor_mul(
                    buf[:, off:512], buf[:, off:512], buf[:, off:512]), inc=None)
                op("vector", lambda e, buf=buf, off=off: e.tensor_mul(
                    buf[:, 512 + off:1024], buf[:, 512 + off:1024],
                    buf[:, 512 + off:1024]), inc="v")
            ready = ("v", C["v"])
        pair_user[jj[0] % 3] = ready
        pat_evac[it] = ready
        if r >= 0:
            op("gpsimd", lambda e, buf=buf, r=r: e.tensor_mul(
                buf[:, r:r + 128], buf[:, r:r + 128], TRIC[:, 0:128]),
               waits=[ready, ("g", cconst_g)], inc=None)
            op("gpsimd", lambda e, buf=buf, r=r: e.tensor_mul(
                buf[:, 512 + r:512 + r + 128], buf[:, 512 + r:512 + r + 128],
                TRIC[:, 0:128]), inc="g")
            ready = ("g", C["g"])
        pat_ready[it] = ready
        jj[0] += 1

    bank_free = {}
    pending = []

    def flush_bcast():
        Q, t, ck, zp, copy_v = pending.pop()
        bcw = [("v", qres["v"]), ("s", rsqb_s[(t + 1) % 2]), ("g", cconst_g),
               copy_v]
        op("tensor", lambda e, t=t, Q=Q, zp=zp: e.matmul(
            zp[:], SPC[:, (7 - t) * 128:(8 - t) * 128], RQH[:, Q * 512:(Q + 1) * 512],
            start=True, stop=False), waits=bcw, inc=None)
        op("tensor", lambda e, t=t, Q=Q, zp=zp: e.matmul(
            zp[:], SPC[:, (7 - t) * 128:(8 - t) * 128], RQL[:, Q * 512:(Q + 1) * 512],
            start=False, stop=True), inc="t")
        bc_t = C["t"]
        op("scalar", lambda e, t=t, zp=zp: e.copy(RSQB[t % 2][:], zp[:]),
           waits=[("t", bc_t), ("v", rsqb_reader[t % 2]), ("v", kres["v"])], inc="s")
        rsqb_s[t % 2] = C["s"]
        bank_free[ck] = ("s", C["s"])
        op("vector", lambda e, t=t, Q=Q: e.tensor_mul(
            QT[t][:, Q * 512:(Q + 1) * 512], QT[t][:, Q * 512:(Q + 1) * 512],
            RSQB[t % 2][:]),
           waits=[("s", rsqb_s[t % 2])], inc="v")
        rsqb_reader[t % 2] = C["v"]

    def emit_z(it):
        Q, t, j = it
        ck = Q * NHP + t
        zp = PZ if ck % 2 == 0 else PX
        nk = 4 * (Q + 1)
        r = j * 128 - Q * 512
        off = max(0, r)
        buf = pat_map[it]
        ready = pat_ready[it]
        zw = [ready, ("v", vttb_v[(j, 0)]), ("v", vttb_v[(j, 1)])]
        if j == 0:
            zw.append(bank_free.get(ck - 2))
            zw.append(zevac_v.get(ck - 2))
            if Q == 0 and t == 0:
                zw.append(("v", v_done_v))
        if j == 3 and pending:
            flush_bcast()
        op("tensor", lambda e, t=t, j=j, off=off, buf=buf, nk=nk, zp=zp: e.matmul(
            zp[0:64, off:512], VT[j][:, t * 128:t * 128 + 64], buf[:, off:512],
            start=(j == 0), stop=(j == nk - 1), tile_position=(0, 0)),
           waits=zw, inc=None)
        op("tensor", lambda e, t=t, j=j, off=off, buf=buf, nk=nk, zp=zp: e.matmul(
            zp[64:128, off:512], VT[j][:, t * 128 + 64:(t + 1) * 128],
            buf[:, 512 + off:1024],
            start=(j == 0), stop=(j == nk - 1), tile_position=(0, 64),
            skip_group_check=True), inc="t")
        patf_user[pat_slot[it]] = ("t", C["t"])
        if j == nk - 1:
            op("scalar", lambda e, t=t, Q=Q, zp=zp: e.copy(
                QT[t][:, Q * 512:(Q + 1) * 512], zp[:]),
               waits=[("t", C["t"])], inc="s")
            zevac_v[ck] = ("s", C["s"])
            pending.append((Q, t, ck, zp, ("s", C["s"])))

    tmps_s = {}
    csw_v_holder = [None]
    for i, it in enumerate(items):
        if 20 <= i < 28:
            op("vector", lambda e, i=i: e.reduce_sum(
                CSWO[:, i - 20:i - 19], WB[i - 20][:], axis=mybir.AxisListType.X),
               waits=[("dO", 1)], inc="v")
        if i == 28:
            op("vector", lambda e: e.tensor_copy(CSWOH[:], CSWO[:]), inc="v")
            op("vector", lambda e: e.tensor_sub(CSWOL[:], CSWO[:], CSWOH[:]), inc="v")
            cswo_v = C["v"]
        if i in (32, 36):
            fi = (i - 32) // 4
            dsem = "df0" if fi % 2 == 0 else "df1"
            wfw = [("t", last_vproj_t), ("v", v_done_v)] if fi == 0 else []
            op("gpsimd" if fi % 2 == 0 else "sync",
               lambda e, fi=fi: e.dma_start(
                   out=WS[fi % 2][:], in_=w_h["wft"][fi * 128:(fi + 1) * 128, :]),
               waits=wfw, inc=dsem)
        if 48 <= i < 80 and i % 4 == 0:
            fi = (i - 48) // 4
            dsem = "df0" if fi % 2 == 0 else "df1"
            op("scalar", lambda e, fi=fi: e.sign(SGNT[:], WS[fi % 2][:]),
               waits=[(dsem, 1), ("v", C["v"])], inc="s")
            tmps_s[fi] = C["s"]
            op("vector", lambda e, fi=fi: e.reduce_sum(
                CSWO[:, fi:fi + 1], SGNT[:], axis=mybir.AxisListType.X),
               waits=[("s", tmps_s[fi]), ("v", cswo_v)], inc="v")
            if fi + 2 < NT:
                fj = fi + 2
                dsem2 = "df0" if fj % 2 == 0 else "df1"
                op("gpsimd" if fj % 2 == 0 else "sync",
                   lambda e, fj=fj: e.dma_start(
                       out=WS[fj % 2][:], in_=w_h["wft"][fj * 128:(fj + 1) * 128, :]),
                   waits=[("s", tmps_s[fi])], inc=dsem2)
        if i == 80:
            op("vector", lambda e: e.tensor_copy(CSWH[:], CSWO[:]), inc="v")
            op("vector", lambda e: e.tensor_sub(CSWL[:], CSWO[:], CSWH[:]), inc="v")
            csw_v_holder[0] = C["v"]
        emit_scores(it)
        if i >= 3:
            emit_z(items[i - 3])
    for it in items[-3:]:
        emit_z(it)
    flush_bcast()
    for fi in range(NT):
        op("gpsimd", lambda e, fi=fi: e.dma_start(
            out=SW[fi][:], in_=w_h["wft"][fi * 128:(fi + 1) * 128, :]),
           waits=[("t", C["t"])] if fi == 0 else (), inc="dsw")
    attn_done_v = C["v"]
    attn_done_t = C["t"]

    if _TAPS:
        for i in range(NT):
            op("sync", lambda e, i=i: e.dma_start(out=taps["tap_z"][i * 128:(i + 1) * 128, :], in_=QT[i][:]),
               waits=[("v", attn_done_v)], inc="dout")

    pw_pool = [PAIR[i][:, h * 512:h * 512 + 512] for i in range(3) for h in range(2)]
    pw_free = {k: pair_user[k // 2] for k in range(6)}
    st34 = dict(ffn_casts=None, csw_v=None, pzrow_free_s=0, mh_guard_t=0,
                m2_guard_t=0, m2chain_v=0, rvchain_v=0, rvb_reader_v=0,
                rvb_reader_g=0, rvb_reader_vm=0, var_free_s=0, rvb_g={})
    u_v, ubf_v, wc_v, m2_v_q, var_t_q = {}, {}, {}, {}, {}
    wcsq_reader_t = [0, 0]
    pwi = [0]

    def next_pw():
        k = pwi[0] % 6
        pwi[0] += 1
        return k

    def stageA(Q, cprev=None):
        xc_sem = []
        for i in range(NT):
            if Q == 0:
                w = [("t", attn_done_t), ("v", attn_done_v)]
            else:
                w = [("v", u_v[(Q - 1, i)])]
            ds = ("dxc", "dxc1", "dxc2", "dxc3")[i % 4]
            op("sync" if i % 2 == 0 else "gpsimd",
               lambda e, i=i, Q=Q: e.dma_start(
                   out=XC[i][:], in_=xt_h[i * 128:(i + 1) * 128, Q * 512:(Q + 1) * 512]),
               waits=w, inc=ds)
            xc_sem.append((ds, 1))
        for i in range(NT):
            w = [("v", attn_done_v), ("v", cswo_v), ("s", st34["pzrow_free_s"])]
            if Q == 0 and i == 0:
                w.append(("s", rsqb_s[0]))
                w.append(("s", rsqb_s[1]))
            op("tensor", lambda e, i=i, Q=Q: e.matmul(
                PX[0:1, 0:512], CSWOH[:, i:i + 1], QT[i][:, Q * 512:(Q + 1) * 512],
                start=(i == 0), stop=False), waits=(w if i == 0 else []), inc=None)
            op("tensor", lambda e, i=i, Q=Q: e.matmul(
                PX[0:1, 0:512], CSWOL[:, i:i + 1], QT[i][:, Q * 512:(Q + 1) * 512],
                start=False, stop=False), inc=None)
        op("tensor", lambda e, Q=Q: e.matmul(
            PX[0:1, 0:512], ONE1[0:1, 0:1], SUM2X[0:1, Q * 512:(Q + 1) * 512],
            start=False, stop=True), inc="t")
        stat_t = C["t"]
        op("scalar", lambda e: e.copy(MROW2[:], PX[0:1, 0:512]),
           waits=[("t", stat_t), ("v", st34["m2chain_v"])], inc="s")
        st34["pzrow_free_s"] = C["s"]
        op("vector", lambda e: e.tensor_copy(MH[:], MROW2[:]),
           waits=[("s", C["s"]), ("t", st34["mh_guard_t"])], inc="v")
        mh_v = C["v"]
        op("vector", lambda e: e.tensor_sub(ML[:], MROW2[:], MH[:]), inc="v")
        m_v = C["v"]
        for ot in range(NT):
            pk = next_pw()
            pw = pw_pool[pk]
            w = [pw_free[pk], ("v", attn_done_v)]
            for i in range(NT):
                op("tensor", lambda e, i=i, ot=ot, Q=Q, pw=pw: e.matmul(
                    pw, WB[i][:, ot * 128:(ot + 1) * 128], QT[i][:, Q * 512:(Q + 1) * 512],
                    start=(i == 0), stop=False),
                   waits=([*w, ("dO", 1)] if i == 0 else [("dO", 1)]),
                   inc=None)
            op("tensor", lambda e, pw=pw: e.matmul(
                pw, NEGM[:], MH[:], start=False, stop=False),
               waits=[("v", mh_v)], inc=None)
            op("tensor", lambda e, pw=pw: e.matmul(
                pw, NEGM[:], ML[:], start=False, stop=True),
               waits=[("v", m_v)], inc="t")
            wo_t = C["t"]
            st34["mh_guard_t"] = wo_t
            op("vector", lambda e, ot=ot, pw=pw: e.scalar_tensor_tensor(
                U[ot][:], XC[ot][:], 2.0, pw, op0=ALU.mult, op1=ALU.add),
               waits=[("t", wo_t), xc_sem[ot]], inc="v")
            u_v[(Q, ot)] = C["v"]
            op("vector", lambda e, ot=ot, Q=Q: e.tensor_copy(
                XB[ot][:, Q * 512:(Q + 1) * 512], U[ot][:]), inc="v")
            ubf_v[(Q, ot)] = C["v"]
            pw_free[pk] = ("v", u_v[(Q, ot)])
            if cprev is not None and ot == 1:
                stageC(cprev)
            if Q >= 1 and ot >= 3:
                emit_b_stat(Q, ot - 3)
        if Q >= 1:
            for i in range(5, NT):
                emit_b_stat(Q, i)
        if True:
            pass
        if _TAPS:
            for i in range(NT):
                op("sync", lambda e, i=i, Q=Q: e.dma_start(
                    out=taps["tap_u"][i * 128:(i + 1) * 128, Q * 512:(Q + 1) * 512],
                    in_=XB[i][:, Q * 512:(Q + 1) * 512]),
                   waits=[("v", ubf_v[(Q, i)])], inc="dout")
        if st34["ffn_casts"] is None:
            ffn_casts_global = []
            for fi in range(NT):
                op("scalar", lambda e, fi=fi: e.sign(SW[fi][:], SW[fi][:]),
                   waits=[("dsw", 1)], inc="s")
                ffn_casts_global.append(C["s"])
            st34["ffn_casts"] = ffn_casts_global
            st34["csw_v"] = csw_v_holder[0]

    def emit_b_stat(Q, i):
        w0 = [("v", st34["csw_v"]), ("s", st34["pzrow_free_s"]), ("v", ubf_v[(Q, i)])]
        wi = [("v", ubf_v[(Q, i)])]
        op("tensor", lambda e, i=i, Q=Q: e.matmul(
            PX[0:1, 0:512], ONEC[:, 0:1], XB[i][:, Q * 512:(Q + 1) * 512],
            start=(i == 0), stop=False), waits=(w0 if i == 0 else wi), inc=None)
        op("tensor", lambda e, i=i, Q=Q: e.matmul(
            PX[0:1, 0:512], CSWH[:, i:i + 1], XB[i][:, Q * 512:(Q + 1) * 512],
            start=False, stop=False), inc=None)
        op("tensor", lambda e, i=i, Q=Q: e.matmul(
            PX[0:1, 0:512], CSWL[:, i:i + 1], XB[i][:, Q * 512:(Q + 1) * 512],
            start=False, stop=(i == NT - 1)),
           inc=("t" if i == NT - 1 else None))
        if i == NT - 1:
            st34["stat2_t"] = C["t"]

    def stageB(Q):
        ffn_casts = st34["ffn_casts"]
        if Q == 0:
            for i in range(NT):
                emit_b_stat(Q, i)
        stat2_t = st34["stat2_t"]
        op("scalar", lambda e: e.copy(MROW[:], PX[0:1, 0:512]),
           waits=[("t", stat2_t)], inc="s")
        st34["pzrow_free_s"] = C["s"]
        op("vector", lambda e: e.tensor_copy(MH2[:], MROW[:]),
           waits=[("s", C["s"]), ("t", st34["m2_guard_t"])], inc="v")
        mh2_v = C["v"]
        op("vector", lambda e: e.tensor_sub(ML2[:], MROW[:], MH2[:]), inc="v")
        m2_v = C["v"]
        st34["m2chain_v"] = m2_v
        vm = {}

        def emit_var(ot):
            wv = [("v", vm[ot]), ("s", st34["var_free_s"]),
                  ("s", st34["rvb_reader_v"]), ("v", attn_done_v)]
            op("tensor", lambda e, ot=ot: e.matmul(
                PZ[0:1, 0:512], ONEC[:, 0:1], WCSQ[ot % 2][:],
                start=(ot == 0), stop=(ot == NT - 1)),
               waits=wv, inc="t")
            wcsq_reader_t[ot % 2] = C["t"]

        for ot in range(NT):
            pk = next_pw()
            pw = pw_pool[pk]
            for i in range(NT):
                op("tensor", lambda e, i=i, ot=ot, Q=Q, pw=pw: e.matmul(
                    pw, SW[i][:, ot * 128:(ot + 1) * 128], XB[i][:, Q * 512:(Q + 1) * 512],
                    start=(i == 0), stop=False),
                   waits=[("s", ffn_casts[i]), ("v", ubf_v[(Q, i)]), pw_free[pk]],
                   inc=None)
            if ot >= 1:
                emit_var(ot - 1)
            op("tensor", lambda e, pw=pw: e.matmul(
                pw, NEGM[:], MH2[:], start=False, stop=False),
               waits=[("v", mh2_v)], inc=None)
            op("tensor", lambda e, pw=pw: e.matmul(
                pw, NEGM[:], ML2[:], start=False, stop=True),
               waits=[("v", m2_v)], inc="t")
            f_t = C["t"]
            st34["m2_guard_t"] = f_t
            op("vector", lambda e, ot=ot, pw=pw: e.scalar_tensor_tensor(
                WCT[ot][:], U[ot][:], 1.0, pw, op0=ALU.mult, op1=ALU.add),
               waits=[("t", f_t), st34["rvb_g"].get(ot)], inc="v")
            wc_v[(Q, ot)] = C["v"]
            pw_free[pk] = ("v", C["v"])
            op("vector", lambda e, ot=ot: e.tensor_mul(
                WCSQ[ot % 2][:], WCT[ot][:], WCT[ot][:]),
               waits=[("t", wcsq_reader_t[ot % 2]), ("s", kres["s"])], inc="v")
            vm[ot] = C["v"]
        emit_var(NT - 1)
        var_t_q[Q] = C["t"]

    def stageC(Q):
        op("scalar", lambda e: e.activation(VROW[:], PZ[0:1, 0:512], AF.Ln,
                                            bias=EPSLC[0:1, 0:1], scale=SCL1K[0:1, 0:1]),
           waits=[("t", var_t_q[Q]), ("v", st34["rvchain_v"])], inc="s")
        st34["var_free_s"] = C["s"]
        op("scalar", lambda e: e.activation(VROW2[:], VROW[:], AF.Exp,
                                            scale=NEGH[0:1, 0:1]), inc="s")
        op("vector", lambda e: e.tensor_copy(RVH[:], VROW2[:]),
           waits=[("s", C["s"])], inc="v")
        rvh_v = C["v"]
        op("vector", lambda e: e.tensor_sub(RVL[:], VROW2[:], RVH[:]), inc="v")
        rv_v = C["v"]
        st34["rvchain_v"] = rv_v
        rvw = [("v", rvh_v), ("s", st34["rvb_reader_v"]), ("s", st34["var_free_s"])]
        if Q == 0:
            rvw.append(("v", attn_done_v))
        op("tensor", lambda e: e.matmul(PZ[:, 0:512], ONE1[:], RVH[:],
                                        start=True, stop=False),
           waits=rvw, inc=None)
        op("tensor", lambda e: e.matmul(PZ[:, 0:512], ONE1[:], RVL[:],
                                        start=False, stop=True),
           waits=[("v", rv_v)], inc="t")
        rvb_t = C["t"]
        op("scalar", lambda e: e.copy(RVB2[:], PZ[:, 0:512]),
           waits=[("t", rvb_t), ("g", st34["rvb_reader_g"]),
                  ("v", st34["rvb_reader_vm"]), ("v", attn_done_v)], inc="s")
        rsvb_s = C["s"]
        st34["rvb_reader_v"] = rsvb_s
        for ot in range(NT):
            dsem = f"do{ot % 3}"
            eng, esem = ("gpsimd", "g") if ot % 2 == 0 else ("vector", "v")
            op(eng, lambda e, ot=ot: e.tensor_mul(
                OUTS[ot % 3][:], WCT[ot][:], RVB2[:]),
               waits=[("s", rsvb_s), (dsem, 1), ("v", wc_v[(Q, ot)])], inc=esem)
            st34["rvb_g"][ot] = (esem, C[esem])
            if esem == "g":
                st34["rvb_reader_g"] = C["g"]
            else:
                st34["rvb_reader_vm"] = C["v"]
            op("sync", lambda e, ot=ot, Q=Q: e.dma_start(
                out=o_h[Q * D + ot * 128:Q * D + (ot + 1) * 128, :], in_=OUTS[ot % 3][:]),
               waits=[(esem, C[esem])], inc=dsem)

    stageA(0)
    stageB(0)
    for Q in range(1, CH):
        stageA(Q, cprev=Q - 1)
        stageB(Q)
    stageC(CH - 1)

    op("sync", lambda e: e.nop(),
       waits=[(s, C[s]) for s in DMA_SEMS if C[s] > 0])

    B.check()
    B.emit()
    return nc


_NC = None
_CONSTS = None


def kernel(L0_attn_x, L0_attn_Wq, L0_attn_bq, L0_attn_Wk, L0_attn_bk,
           L0_attn_Wv, L0_attn_bv, L0_attn_Wo, L0_attn_norm_w, L0_ffn_weight):
    global _NC, _CONSTS
    if _NC is None:
        _NC = build_nc()
    if _CONSTS is None:
        _CONSTS = _consts()
    x = np.asarray(L0_attn_x, dtype=np.float32)
    nw = np.asarray(L0_attn_norm_w, dtype=np.float32)
    idx = np.arange(128) % 64
    shared = dict(
        wqt=np.ascontiguousarray(np.asarray(L0_attn_Wq, np.float32).T),
        wkt=np.ascontiguousarray(np.asarray(L0_attn_Wk, np.float32).T),
        wvt=np.ascontiguousarray(np.asarray(L0_attn_Wv, np.float32).T),
        wot=np.ascontiguousarray(np.asarray(L0_attn_Wo, np.float32).T),
        wft=np.ascontiguousarray(np.asarray(L0_ffn_weight, np.float32).T),
        bq=np.ascontiguousarray(np.asarray(L0_attn_bq, np.float32)),
        bk=np.ascontiguousarray(np.asarray(L0_attn_bk, np.float32)),
        bv=np.ascontiguousarray(np.asarray(L0_attn_bv, np.float32)),
        nw2=np.ascontiguousarray(nw[idx]),
        nwp2=np.ascontiguousarray(nw[idx ^ 1]),
        **_CONSTS,
    )
    in_maps = []
    for c in range(8):
        m = dict(shared)
        m["xt"] = np.ascontiguousarray(x[c].T)
        in_maps.append(m)
    res = run_bass_kernel_spmd(_NC, in_maps, core_ids=list(range(8)))
    def reasm(a):
        a = np.asarray(a).reshape(CH, D, 512)
        return np.concatenate(list(a), axis=1).T
    out = np.stack([np.ascontiguousarray(reasm(r["o"])) for r in res.results], axis=0)
    if _TAPS:
        kernel._last_taps = res.results
    return out.astype(np.float32)
